# version 18
# baseline (speedup 1.0000x reference)
import numpy as np
from contextlib import ExitStack

import concourse.bass as bass
import concourse.mybir as mybir
from concourse.bass_utils import run_bass_kernel_spmd

F32 = mybir.dt.float32
BF16 = mybir.dt.bfloat16
AF = mybir.ActivationFunctionType
ALU = mybir.AluOpType
AX = mybir.AxisListType

CFG_FULL = dict(D=1024, T=2048, debug=())
CFG_SMALL = dict(D=256, T=512, debug=())

DS = 64
DCONV = 4
NE = 8
EPS_LN = 1e-5
EPS_A = 1e-4
ENAMES = ("t", "v", "s", "sy", "g")


class Ctr:

    def __init__(self, sem):
        self.sem = sem
        self.n = 0


class Stream:

    def __init__(self, prog, ename):
        self.prog = prog
        self.ename = ename
        self.ops = []
        self.last_op = None

    def emit(self, fn, inc=None):
        val = None
        incs = []
        if inc is not None:
            inc.n += 1
            val = inc.n
            incs.append(inc.sem)
        idx = len(self.ops)
        self.ops.append(self._wrap(fn, incs))
        self.last_op = (idx, fn, incs)
        return val

    @staticmethod
    def _wrap(fn, incs):
        if not incs:
            def run(eng, fn=fn):
                fn(eng)
        else:
            def run(eng, fn=fn, incs=tuple(incs)):
                inst = fn(eng)
                for sem in incs:
                    inst.then_inc(sem, 1)
        return run

    def attach_inc(self, sem):
        if (self.last_op is not None and self.last_op[0] == len(self.ops) - 1
                and len(self.last_op[2]) < 1):
            idx, fn, incs = self.last_op
            incs.append(sem)
            self.ops[idx] = self._wrap(fn, incs)
        else:
            fn = lambda eng: eng.drain()
            incs = [sem]
            self.ops.append(self._wrap(fn, incs))
            self.last_op = (len(self.ops) - 1, fn, incs)

    def wait(self, ctr, val):
        if val is None or val <= 0:
            return
        key = (self.ename, id(ctr))
        floors = self.prog.floors
        if floors.get(key, 0) >= val:
            return
        floors[key] = val
        sem = ctr.sem if isinstance(ctr, Ctr) else ctr
        self.ops.append(lambda eng, sem=sem, val=val: eng.wait_ge(sem, val))
        self.last_op = None

    def wait_dma(self, sem, val):
        if val <= 0:
            return
        key = (self.ename, id(sem))
        if self.prog.floors.get(key, 0) >= val:
            return
        self.prog.floors[key] = val
        self.ops.append(lambda eng, sem=sem, val=val: eng.wait_ge(sem, val))
        self.last_op = None

    def drain(self):
        fn = lambda eng: eng.drain()
        self.ops.append(self._wrap(fn, []))
        self.last_op = (len(self.ops) - 1, fn, [])

    def bar(self, ctr):
        ctr.n += 1
        self.attach_inc(ctr.sem)
        return ctr.n


class Prog:
    def __init__(self, nc):
        self.nc = nc
        self.floors = {}
        self._new_streams()

    def _new_streams(self):
        for en in ENAMES:
            setattr(self, en, Stream(self, en))

    def flush(self, name):
        streams = {en: getattr(self, en) for en in ENAMES}
        with self.nc.Block(name, no_gpsimd_drain=True) as blk:
            for en, dec in (("t", blk.tensor), ("v", blk.vector), ("s", blk.scalar),
                            ("sy", blk.sync), ("g", blk.gpsimd)):
                ops = streams[en].ops
                if ops:
                    dec(lambda eng, ops=ops: [f(eng) for f in ops])
        self._new_streams()


class Bank:

    def __init__(self, t):
        self.t = t
        self.guards = []

    def acquire(self, st):
        for ctr, val in self.guards:
            st.wait(ctr, val)
        self.guards = []

    def guard(self, ctr, val=None):
        self.guards.append((ctr, val if val is not None else ctr.n))


class Ring:

    def __init__(self, nc, ctx, nslots, slot_elems):
        self.nslots = nslots
        self.slot_elems = slot_elems
        self.slots = [ctx.enter_context(nc.sbuf_tensor(f"ring{i}", [128, slot_elems], BF16))
                      for i in range(nslots)]
        self.ld = [ctx.enter_context(nc.semaphore(f"rld{i}")) for i in range(nslots)]
        self.fr = [ctx.enter_context(nc.semaphore(f"rfr{i}")) for i in range(nslots)]
        self.nload = 0
        self.nfree = [0] * nslots

    def load(self, p, src2d, prows, kc, cw):
        i = self.nload
        self.nload += 1
        s = i % self.nslots
        gen = i // self.nslots
        assert kc * cw <= self.slot_elems
        if gen > 0:
            p.g.wait_dma(self.fr[s], gen)
        src = src2d.rearrange("(k p) m -> p k m", p=prows)
        slot = self.slots[s]

        def fn(eng, src=src, slot=slot, prows=prows, kc=kc, cw=cw, sem=self.ld[s]):
            dst = slot[0:prows, 0:kc * cw].rearrange("p (k m) -> p k m", k=kc)
            eng.dma_start(out=dst, in_=src).then_inc(sem, 16)
        p.g.ops.append(fn)
        return dict(s=s, gen=gen, kc=kc, cw=cw, prows=prows)

    def slot(self, h):
        return self.slots[h["s"]]

    def use(self, st, h):
        st.wait_dma(self.ld[h["s"]], 16 * (h["gen"] + 1))

    def lhsT(self, h, k, m0, mw):
        return self.slots[h["s"]][0:h["prows"], k * h["cw"] + m0: k * h["cw"] + m0 + mw]

    def release(self, st, h):
        self.nfree[h["s"]] += 1

        st.attach_inc(self.fr[h["s"]])


def build(cfg):
    D = cfg["D"]
    T = cfg["T"]
    KT = D // 128
    CW = min(512, T)
    NCH = T // CW
    NTT = T // 128
    NH = D // 64
    NPAIR = NH // 2
    F1 = 4 * D
    PROJ = 5 * DS
    debug = set(cfg.get("debug", ()))
    stop_after = cfg.get("stop_after", "")

    nc = bass.Bass("TRN2", target_bir_lowering=False)

    d_x = nc.dram_tensor("xt", [D, T], F32, kind="ExternalInput").ap()
    d_y = nc.dram_tensor("yt", [D, T], F32, kind="ExternalOutput").ap()
    d_convw = nc.dram_tensor("convw", [128, KT * DCONV], F32, kind="ExternalInput").ap()
    d_convb = nc.dram_tensor("convb", [128, KT], F32, kind="ExternalInput").ap()
    d_inpw = nc.dram_tensor("inpw", [D, PROJ], F32, kind="ExternalInput").ap()
    d_inpb = nc.dram_tensor("inpb", [128, 3], F32, kind="ExternalInput").ap()
    d_alog = nc.dram_tensor("alog", [DS, 1], F32, kind="ExternalInput").ap()
    d_outpw = nc.dram_tensor("outpw", [DS, D], F32, kind="ExternalInput").ap()
    d_outpb = nc.dram_tensor("outpb", [1, D], F32, kind="ExternalInput").ap()
    d_gatew = nc.dram_tensor("gatew", [D, NE], F32, kind="ExternalInput").ap()
    d_gbias = nc.dram_tensor("gbias_tok", [128, NE], F32, kind="ExternalInput").ap()
    d_routerw = nc.dram_tensor("routerw", [D, NE], F32, kind="ExternalInput").ap()
    d_moe1 = nc.dram_tensor("moe1", [10 * D, D], F32, kind="ExternalInput").ap()
    d_moe3 = nc.dram_tensor("moe3", [10 * D, D], F32, kind="ExternalInput").ap()
    d_moe2 = nc.dram_tensor("moe2", [10 * D, D], F32, kind="ExternalInput").ap()
    d_wq = nc.dram_tensor("wq", [D, D], F32, kind="ExternalInput").ap()
    d_wk = nc.dram_tensor("wk", [D, D], F32, kind="ExternalInput").ap()
    d_wv = nc.dram_tensor("wv", [D, D], F32, kind="ExternalInput").ap()
    d_wo = nc.dram_tensor("wo", [D, D], F32, kind="ExternalInput").ap()
    d_w1 = nc.dram_tensor("l1w1", [NE * D, F1], F32, kind="ExternalInput").ap()
    d_b1 = nc.dram_tensor("l1b1", [128, NE * (F1 // 128)], F32, kind="ExternalInput").ap()
    d_w2 = nc.dram_tensor("l1w2", [NE * F1, D], F32, kind="ExternalInput").ap()
    d_b2sel = nc.dram_tensor("b2sel", [NE * NE, D], F32, kind="ExternalInput").ap()

    d_dbg = {}

    def dbg_tensor(name, shape):
        d_dbg[name] = nc.dram_tensor("dbg_" + name, shape, F32, kind="ExternalOutput").ap()
        return d_dbg[name]

    n_out = [1]
    n_dbgdma = [0]

    with ExitStack() as top:
        et = top.enter_context
        x_sb = et(nc.sbuf_tensor("x_sb", [128, KT * T], F32))
        xn = et(nc.sbuf_tensor("xn", [128, KT * T], BF16))
        rowi = et(nc.sbuf_tensor("rowi", [128, 128], BF16))
        identf = et(nc.sbuf_tensor("identf", [128, 128], F32))
        coli = et(nc.sbuf_tensor("coli", [128, 128], BF16))
        ident = et(nc.sbuf_tensor("ident", [128, 128], BF16))
        ebc = et(nc.sbuf_tensor("ebc", [8, NE * 128], BF16))
        sel0f = et(nc.sbuf_tensor("sel0f", [2, 128], F32))
        onesD = et(nc.sbuf_tensor("onesD", [128, 128], BF16))
        onesrow = et(nc.sbuf_tensor("onesrow", [2, CW], BF16))
        convw = et(nc.sbuf_tensor("convw_s", [128, KT * DCONV], F32))
        convb = et(nc.sbuf_tensor("convb_s", [128, KT], F32))
        inpb = et(nc.sbuf_tensor("inpb_s", [128, 3], F32))
        alog = et(nc.sbuf_tensor("alog_s", [DS, 1], F32))
        outpw_b = et(nc.sbuf_tensor("outpw_b", [DS, D], BF16))
        outpb_row = et(nc.sbuf_tensor("outpb_row", [1, D], BF16))
        gatew_b = et(nc.sbuf_tensor("gatew_b", [128, KT * NE], BF16))
        routerw_b = et(nc.sbuf_tensor("routerw_b", [128, KT * NE], BF16))
        gbias_tok = et(nc.sbuf_tensor("gbias_tok_s", [128, NE], F32))
        b1_sb = et(nc.sbuf_tensor("b1_sb", [128, NE * (F1 // 128)], F32))
        b2cur = [et(nc.sbuf_tensor(f"b2cur{i}", [8, D], BF16)) for i in range(2)]
        gt = et(nc.sbuf_tensor("gt", [8, T], BF16))
        NBLK0 = T // 64
        negE = et(nc.sbuf_tensor("negE", [128, NBLK0 * 64], BF16))
        fconst = et(nc.sbuf_tensor("fconst", [128, 16], F32))
        fconst_vals = {}

        def fc(val):
            assert val in fconst_vals, f"const {val} not pre-registered"
            return fconst[:, fconst_vals[val]:fconst_vals[val] + 1]

        for _v in (EPS_LN, 0.125, 0.5, 1.0 / 6.0):
            fconst_vals[_v] = len(fconst_vals)
        gb_sb = et(nc.sbuf_tensor("gb_sb", [128, T], BF16))

        mm = Ctr(et(nc.semaphore("mm")))
        ve = Ctr(et(nc.semaphore("ve")))
        sc = Ctr(et(nc.semaphore("sc")))
        gp = Ctr(et(nc.semaphore("gp")))
        s_ldx = et(nc.semaphore("s_ldx"))
        s_ldg = et(nc.semaphore("s_ldg"))
        s_b2 = et(nc.semaphore("s_b2"))
        s_b2f = Ctr(et(nc.semaphore("s_b2f")))
        s_out = et(nc.semaphore("s_out"))
        s_dbg = et(nc.semaphore("s_dbg"))

        ring = Ring(nc, top, nslots=2, slot_elems=4096)
        p = Prog(nc)

        def dbg_fence():
            if n_dbgdma[0]:
                for st in (p.v, p.s, p.t):
                    st.wait_dma(s_dbg, 16 * n_dbgdma[0])

        def dump_dbg(name, sb, shape, ctrval, prows=128):
            dd = dbg_tensor(name, shape)
            n_out[0] += 1
            n_dbgdma[0] += 1
            ctr, val = ctrval
            p.g.wait(ctr, val)
            kk = shape[0] // prows

            def dump(g, dd=dd, sb=sb, kk=kk, prows=prows):
                if kk > 1:
                    g.dma_start(out=dd.rearrange("(k p) t -> p k t", p=prows),
                                in_=sb[:].rearrange("p (k t) -> p k t", k=kk)
                                ).then_inc(s_dbg, 16)
                else:
                    g.dma_start(out=dd, in_=sb[0:prows, :]).then_inc(s_dbg, 16)
            p.g.ops.append(dump)

        n_ldx = 7

        def _init_sync(s):
            s.dma_start(out=x_sb[:].rearrange("p (k t) -> p k t", k=KT),
                        in_=d_x.rearrange("(k p) t -> p k t", p=128)).then_inc(s_ldx, 16)
            for dst, src in [(convw, d_convw), (convb, d_convb), (inpb, d_inpb),
                             (alog, d_alog), (gbias_tok, d_gbias), (b1_sb, d_b1)]:
                s.dma_start(out=dst[:], in_=src).then_inc(s_ldx, 16)
        p.sy.ops.append(_init_sync)

        n_ldg = 4

        def _init_g(g):
            for dst, src, kk in [(gatew_b, d_gatew, KT), (routerw_b, d_routerw, KT)]:
                g.dma_start(out=dst[:].rearrange("p (k m) -> p k m", k=kk),
                            in_=src.rearrange("(k p) m -> p k m", p=128)).then_inc(s_ldg, 16)
            g.dma_start(out=outpw_b[:], in_=d_outpw).then_inc(s_ldg, 16)
            g.dma_start(out=outpb_row[:], in_=d_outpb).then_inc(s_ldg, 16)
            g.iota(rowi[:, :], [[0, 128]], channel_multiplier=1,
                   allow_small_or_imprecise_dtypes=True)
            g.iota(coli[:, :], [[1, 128]], channel_multiplier=0,
                   allow_small_or_imprecise_dtypes=True)
            g.drain()
        p.g.ops.append(_init_g)
        p.g.bar(gp)
        GP_IOTA = gp.n

        p.v.wait(gp, GP_IOTA)

        def _init_v(v):
            v.tensor_tensor(ident[:], rowi[:], coli[:], ALU.is_equal)
            v.tensor_tensor(identf[:], rowi[:], coli[:], ALU.is_equal)
            for e in range(NE):
                v.tensor_scalar(ebc[:, e * 128:(e + 1) * 128], rowi[0:8, :],
                                float(e), None, ALU.is_equal)
            v.memset(sel0f[:, :], 0.0)
            v.drain()
            v.memset(sel0f[0:1, :], 1.0)
            v.memset(onesD[:], 1.0 / D)
            v.memset(onesrow[:], 0.0)
            v.drain()
            v.memset(onesrow[0:1, :], 1.0)
            for val, i in fconst_vals.items():
                v.memset(fconst[:, i:i + 1], float(val))
            v.drain()
        p.v.ops.append(_init_v)
        p.v.bar(ve)
        VE_CONST = ve.n

        with nc.sbuf_tensor("blkx_tmp", [128, NBLK0 * 64], BF16) as blkx, \
                nc.sbuf_tensor("rmod_tmp", [128, 1], BF16) as rmod:
            p.g.ops.append(lambda g, blkx=blkx: g.iota(
                blkx[:, :], [[1, NBLK0], [0, 64]], channel_multiplier=0,
                allow_small_or_imprecise_dtypes=True))
            gp_bx = p.g.bar(gp)
            p.v.wait(gp, gp_bx)
            p.v.emit(lambda v, rmod=rmod: v.tensor_scalar(
                rmod[:], rowi[:, 0:1], 64.0, None, ALU.is_ge))
            p.v.drain()
            p.v.emit(lambda v, rmod=rmod: v.scalar_tensor_tensor(
                rmod[:], rmod[:], -64.0, rowi[:, 0:1], ALU.mult, ALU.add))
            p.v.drain()
            p.v.emit(lambda v, blkx=blkx, rmod=rmod: v.tensor_tensor(
                negE[:], bass.AP(rmod, 0, [[1, 128], [0, NBLK0 * 64]]),
                blkx[:], ALU.is_equal))
            p.v.drain()
            p.v.emit(lambda v: v.tensor_scalar(negE[:], negE[:], -1.0, None, ALU.mult))
            p.v.bar(ve)
        VE_CONST = ve.n

        def wait_x(st):
            st.wait_dma(s_ldx, 16 * n_ldx)

        def wait_consts(st):
            st.wait_dma(s_ldg, 16 * n_ldg)
            st.wait(ve, VE_CONST)

        def xv(k, c, w=None):
            w = CW if w is None else w
            return x_sb[:, k * T + c * CW: k * T + c * CW + w]

        def xnv(k, c, w=None):
            w = CW if w is None else w
            return xn[:, k * T + c * CW: k * T + c * CW + w]

        p.flush("init")

        def ln_phase(lname):
            dbg_fence()
            with ExitStack() as ctx:
                e = ctx.enter_context
                xstage = [e(nc.sbuf_tensor(f"{lname}_xs{i}", [128, KT * CW], BF16)) for i in range(2)]
                sqstage = [e(nc.sbuf_tensor(f"{lname}_qs{i}", [128, KT * CW], BF16)) for i in range(2)]
                m_sb = [e(nc.sbuf_tensor(f"{lname}_m{i}", [128, CW], F32)) for i in range(2)]
                vtmp = [e(nc.sbuf_tensor(f"{lname}_v{i}", [128, CW], F32)) for i in range(2)]
                xtmp = [e(nc.sbuf_tensor(f"{lname}_t{i}", [128, CW], F32)) for i in range(2)]
                pm = [Bank(e(nc.psum_tensor(f"{lname}_pm{i}", [128, CW], F32))) for i in range(2)]
                ps2 = [Bank(e(nc.psum_tensor(f"{lname}_ps{i}", [128, CW], F32))) for i in range(2)]
                stage_guard = [None, None]
                msb_guard = [None, None]

                for c in range(NCH):
                    b = c % 2
                    wait_x(p.v)
                    if stage_guard[b] is not None:
                        p.v.wait(mm, stage_guard[b])
                    for k in range(KT):
                        p.v.emit(lambda v, k=k, b=b, c=c: v.tensor_copy(
                            xstage[b][:, k * CW:(k + 1) * CW], xv(k, c)))
                    p.v.drain()
                    for k in range(KT):
                        p.v.emit(lambda v, k=k, b=b: v.tensor_mul(
                            sqstage[b][:, k * CW:(k + 1) * CW],
                            xstage[b][:, k * CW:(k + 1) * CW],
                            xstage[b][:, k * CW:(k + 1) * CW]))
                    p.v.drain()
                    ve_stage = p.v.bar(ve)

                    wait_consts(p.t)
                    p.t.wait(ve, ve_stage)
                    pm[b].acquire(p.t)
                    ps2[b].acquire(p.t)
                    for k in range(KT):
                        p.t.emit(lambda t, k=k, b=b: t.matmul(
                            pm[b].t[:], onesD[:], xstage[b][:, k * CW:(k + 1) * CW],
                            start=(k == 0), stop=(k == KT - 1)))
                    for k in range(KT):
                        p.t.emit(lambda t, k=k, b=b: t.matmul(
                            ps2[b].t[:], onesD[:], sqstage[b][:, k * CW:(k + 1) * CW],
                            start=(k == 0), stop=(k == KT - 1)))
                    mm_stat = p.t.bar(mm)
                    stage_guard[b] = mm_stat

                    p.s.wait(mm, mm_stat)
                    if msb_guard[b] is not None:
                        p.s.wait(ve, msb_guard[b])
                    p.s.emit(lambda s, b=b: s.activation(m_sb[b][:], pm[b].t[:], AF.Copy))
                    p.s.emit(lambda s, b=b: s.activation(vtmp[b][:], pm[b].t[:], AF.Square))
                    p.s.drain()
                    sc_m = p.s.bar(sc)

                    p.v.wait(sc, sc_m)
                    p.v.emit(lambda v, b=b: v.tensor_sub(vtmp[b][:], ps2[b].t[:], vtmp[b][:]))
                    p.v.drain()
                    ve_v = p.v.bar(ve)
                    ps2[b].guard(ve, ve_v)
                    pm[b].guard(sc, sc_m)

                    p.s.wait(ve, ve_v)
                    p.s.emit(lambda s, b=b: s.activation(vtmp[b][:], vtmp[b][:], AF.Sqrt, bias=fc(EPS_LN)))
                    p.s.drain()
                    sc_sd = p.s.bar(sc)

                    p.v.wait(sc, sc_sd)
                    p.v.emit(lambda v, b=b: v.reciprocal(vtmp[b][:], vtmp[b][:]))
                    p.v.drain()
                    for k in range(KT):
                        p.v.emit(lambda v, k=k, b=b, c=c: v.tensor_sub(
                            xtmp[b][:], xv(k, c), m_sb[b][:]))
                        p.v.drain()
                        p.v.emit(lambda v, k=k, b=b, c=c: v.tensor_mul(
                            xnv(k, c), xtmp[b][:], vtmp[b][:]))
                        p.v.drain()
                    ve_done = p.v.bar(ve)
                    msb_guard[b] = ve_done

                if "xn" + lname[-1] in debug:
                    dump_dbg("xn" + lname[-1], xn, [D, T], (ve, ve.n))
                p.flush(lname)

        ln_phase("ln0")
        _stop = [False]

        def stopped(name):
            if _stop[0] or stop_after == name:
                _stop[0] = True
            return _stop[0] and stop_after != name

        def routing_phase(lname, wsb, mode):
            dbg_fence()
            with ExitStack() as ctx:
                e = ctx.enter_context
                lgb = e(nc.sbuf_tensor(f"{lname}_lgb", [8, T], BF16))
                NC8 = NTT * NE
                lg_tok = e(nc.sbuf_tensor(f"{lname}_lgt", [128, NC8], F32))
                w_a = e(nc.sbuf_tensor(f"{lname}_wa", [128, NC8], F32))
                w_b = e(nc.sbuf_tensor(f"{lname}_wb", [128, NC8], F32))
                w_m = e(nc.sbuf_tensor(f"{lname}_wm", [128, NTT], F32))
                w_m2 = e(nc.sbuf_tensor(f"{lname}_wm2", [128, NTT], F32))
                gates_tok = e(nc.sbuf_tensor(f"{lname}_gk", [128, NC8], BF16))
                w_tmp = e(nc.sbuf_tensor(f"{lname}_wt", [128, NC8], F32))
                plg = Bank(e(nc.psum_tensor(f"{lname}_plg", [8, CW], F32)))
                ptr = [Bank(e(nc.psum_tensor(f"{lname}_ptr{i}", [128, NE], BF16))) for i in range(2)]
                ptr2 = [Bank(e(nc.psum_tensor(f"{lname}_pt2{i}", [8, 128], BF16))) for i in range(2)]

                def v3(tsb, last=NE):
                    return bass.AP(tsb, 0, [[NC8 if last == NE else NTT, 128],
                                            [last, NTT], [1, last]])

                def vb(tsb):
                    return bass.AP(tsb, 0, [[NTT, 128], [1, NTT], [0, NE]])

                for c in range(NCH):
                    wait_consts(p.t)
                    plg.acquire(p.t)
                    for k in range(KT):
                        p.t.emit(lambda t, k=k, c=c: t.matmul(
                            plg.t[:], wsb[:, k * NE:(k + 1) * NE], xnv(k, c),
                            start=(k == 0), stop=(k == KT - 1)))
                    mm_lg = p.t.bar(mm)
                    p.s.wait(mm, mm_lg)
                    p.s.emit(lambda s, c=c: s.activation(
                        lgb[:, c * CW:(c + 1) * CW], plg.t[:], AF.Copy))
                    sc_lg = p.s.bar(sc)
                    plg.guard(sc, sc_lg)
                for tt in range(NTT):
                    b = tt % 2
                    p.t.wait(sc, sc.n)
                    ptr[b].acquire(p.t)
                    p.t.emit(lambda t, tt=tt, b=b: t.transpose(
                        ptr[b].t[:], lgb[:, tt * 128:(tt + 1) * 128], ident[0:8, 0:8]))
                    mm_tr = p.t.bar(mm)
                    p.v.wait(mm, mm_tr)
                    p.v.emit(lambda v, tt=tt, b=b: v.tensor_copy(
                        lg_tok[:, tt * NE:(tt + 1) * NE], ptr[b].t[:]))
                    ve_tr = p.v.bar(ve)
                    ptr[b].guard(ve, ve_tr)
                V, S = p.v, p.s
                V.drain()
                V.emit(lambda v: v.tensor_reduce(v3(w_m, 1), v3(lg_tok), axis=AX.X, op=ALU.max))
                V.drain()
                if mode == "l0":
                    V.emit(lambda v: v.tensor_tensor(w_a[:], lg_tok[:], vb(w_m), ALU.subtract))
                    ve_d = V.bar(ve)
                    S.wait(ve, ve_d)
                    S.emit(lambda s: s.activation(w_a[:], w_a[:], AF.Exp))
                    sc_e = S.bar(sc)
                    V.wait(sc, sc_e)
                    V.emit(lambda v: v.tensor_reduce(v3(w_m, 1), v3(w_a), axis=AX.X, op=ALU.add))
                    V.drain()
                    V.emit(lambda v: v.reciprocal(w_m[:], w_m[:]))
                    V.drain()
                    V.emit(lambda v: v.tensor_tensor(w_a[:], w_a[:], vb(w_m), ALU.mult))
                    V.drain()
                    V.emit(lambda v: v.tensor_tensor(
                        w_b[:], w_a[:],
                        bass.AP(gbias_tok, 0, [[NE, 128], [0, NTT], [1, NE]]), ALU.add))
                    V.drain()
                    sel = w_b
                else:
                    sel = lg_tok
                V.emit(lambda v: v.tensor_reduce(v3(w_m, 1), v3(sel), axis=AX.X, op=ALU.max))
                V.drain()
                V.emit(lambda v: v.tensor_tensor(w_tmp[:], sel[:], vb(w_m), ALU.is_equal))
                V.drain()
                V.emit(lambda v: v.scalar_tensor_tensor(w_tmp[:], w_tmp[:], -1e30, sel[:],
                                                        ALU.mult, ALU.add))
                V.drain()
                V.emit(lambda v: v.tensor_reduce(v3(w_m2, 1), v3(w_tmp), axis=AX.X, op=ALU.max))
                V.drain()
                V.emit(lambda v: v.tensor_tensor(w_tmp[:], sel[:], vb(w_m2), ALU.is_ge))
                V.drain()
                if mode == "l0":
                    V.emit(lambda v: v.tensor_tensor(gates_tok[:], w_a[:], w_tmp[:], ALU.mult))
                    V.drain()
                else:
                    V.emit(lambda v: v.tensor_tensor(w_a[:], lg_tok[:], vb(w_m), ALU.subtract))
                    ve_d = V.bar(ve)
                    S.wait(ve, ve_d)
                    S.emit(lambda s: s.activation(w_a[:], w_a[:], AF.Exp))
                    sc_e = S.bar(sc)
                    V.wait(sc, sc_e)
                    V.emit(lambda v: v.tensor_tensor(w_a[:], w_a[:], w_tmp[:], ALU.mult))
                    V.drain()
                    V.emit(lambda v: v.tensor_reduce(v3(w_m, 1), v3(w_a), axis=AX.X, op=ALU.add))
                    V.drain()
                    V.emit(lambda v: v.reciprocal(w_m[:], w_m[:]))
                    V.drain()
                    V.emit(lambda v: v.tensor_tensor(gates_tok[:], w_a[:], vb(w_m), ALU.mult))
                    V.drain()
                ve_gk = V.bar(ve)
                for tt in range(NTT):
                    b = tt % 2
                    p.t.wait(ve, ve_gk)
                    ptr2[b].acquire(p.t)
                    p.t.emit(lambda t, tt=tt, b=b: t.transpose(
                        ptr2[b].t[:], gates_tok[:, tt * NE:(tt + 1) * NE], ident[:]))
                    mm_tb = p.t.bar(mm)
                    p.v.wait(mm, mm_tb)
                    p.v.emit(lambda v, tt=tt, b=b: v.tensor_copy(
                        gt[:, tt * 128:(tt + 1) * 128], ptr2[b].t[:]))
                    ve_tb = p.v.bar(ve)
                    ptr2[b].guard(ve, ve_tb)
                if "gt" + lname[-1] in debug:
                    dump_dbg("gt" + lname[-1], gt, [8, T], (ve, ve.n), prows=8)
                p.flush(lname)

        Th = T // 2
        if stop_after == "ln0":
            pass
        with ExitStack() as l0ctx:
            el0 = l0ctx.enter_context
            pj = [el0(nc.sbuf_tensor(f"pj{i}", [128, T], F32)) for i in range(3)]

            with ExitStack() as ctx:
                e = ctx.enter_context
                xcbh = e(nc.sbuf_tensor("xcbh", [128, KT * Th], BF16))
                acc = e(nc.sbuf_tensor("acc", [128, Th], F32))
                pjb = [Bank(e(nc.psum_tensor(f"pjb{i}", [128, CW], F32))) for i in range(6)]
                h_inpw = ring.load(p, d_inpw, 128, KT, PROJ)
                MWS = [128, 128, 64]
                sc_conv = [0] * KT
                last_acc = [None]
                xcb_guard = [None]
                for half in range(2):
                    c0 = half * Th
                    c1 = c0 + Th
                    if xcb_guard[0] is not None:
                        p.v.wait(mm, xcb_guard[0])
                        p.s.wait(mm, xcb_guard[0])
                    for k in range(KT):
                        cwv = convw[:, k * DCONV: (k + 1) * DCONV]
                        xk = lambda a, b, k=k: xn[:, k * T + a: k * T + b]
                        p.v.drain()
                        p.v.emit(lambda v, k=k, c0=c0, c1=c1, cwv=cwv: v.tensor_scalar(
                            acc[:], xk(c0, c1, k), cwv[:, 1:2], None, ALU.mult))
                        p.v.drain()
                        if c0 == 0:
                            p.v.emit(lambda v, k=k, c1=c1, cwv=cwv: v.scalar_tensor_tensor(
                                acc[:, 1:Th], xk(0, c1 - 1, k), cwv[:, 0:1],
                                acc[:, 1:Th], ALU.mult, ALU.add))
                        else:
                            p.v.emit(lambda v, k=k, c0=c0, c1=c1, cwv=cwv: v.scalar_tensor_tensor(
                                acc[:], xk(c0 - 1, c1 - 1, k), cwv[:, 0:1],
                                acc[:], ALU.mult, ALU.add))
                        p.v.drain()
                        if c1 == T:
                            p.v.emit(lambda v, k=k, c0=c0, cwv=cwv: v.scalar_tensor_tensor(
                                acc[:, 0:Th - 1], xk(c0 + 1, T, k), cwv[:, 2:3],
                                acc[:, 0:Th - 1], ALU.mult, ALU.add))
                            p.v.drain()
                            p.v.emit(lambda v, k=k, c0=c0, cwv=cwv: v.scalar_tensor_tensor(
                                acc[:, 0:Th - 2], xk(c0 + 2, T, k), cwv[:, 3:4],
                                acc[:, 0:Th - 2], ALU.mult, ALU.add))
                        else:
                            p.v.emit(lambda v, k=k, c0=c0, c1=c1, cwv=cwv: v.scalar_tensor_tensor(
                                acc[:], xk(c0 + 1, c1 + 1, k), cwv[:, 2:3],
                                acc[:], ALU.mult, ALU.add))
                            p.v.drain()
                            p.v.emit(lambda v, k=k, c0=c0, c1=c1, cwv=cwv: v.scalar_tensor_tensor(
                                acc[:], xk(c0 + 2, c1 + 2, k), cwv[:, 3:4],
                                acc[:], ALU.mult, ALU.add))
                        p.v.drain()
                        p.v.emit(lambda v, k=k: v.tensor_scalar(
                            xcbh[:, k * Th:(k + 1) * Th], acc[:],
                            convb[:, k:k + 1], None, ALU.add))
                        ve_conv = p.v.bar(ve)
                        sc_conv[k] = ve_conv
                        last_acc[0] = ve_conv
                        if c1 == T:
                            p.v.drain()
                            p.v.emit(lambda v, k=k: v.memset(
                                xcbh[:, (k + 1) * Th - 1:(k + 1) * Th], 0.0))
                            p.v.drain()
                    ve_xcb = p.v.bar(ve)
                    CWH = min(CW, Th)
                    for ch in range(Th // CWH):
                        for m in range(3):
                            bank = pjb[(ch % 2) * 3 + m]
                            ring.use(p.t, h_inpw)
                            p.t.wait(ve, ve_xcb)
                            bank.acquire(p.t)
                            mw = MWS[m]
                            for k in range(KT):
                                p.t.emit(lambda t, k=k, m=m, ch=ch, bank=bank, mw=mw, CWH=CWH: t.matmul(
                                    bank.t[0:mw, 0:CWH], ring.lhsT(h_inpw, k, m * 128, mw),
                                    xcbh[:, k * Th + ch * CWH: k * Th + (ch + 1) * CWH],
                                    start=(k == 0), stop=(k == KT - 1)))
                            mm_pj = p.t.bar(mm)
                            p.v.wait(mm, mm_pj)
                            gcol = half * Th + ch * CWH
                            p.v.emit(lambda v, m=m, bank=bank, mw=mw, gcol=gcol, CWH=CWH: v.tensor_scalar(
                                pj[m][0:mw, gcol:gcol + CWH], bank.t[0:mw, 0:CWH],
                                inpb[0:mw, m:m + 1], None, ALU.add))
                            ve_pj = p.v.bar(ve)
                            bank.guard(ve, ve_pj)
                    xcb_guard[0] = mm.n
                ring.release(p.t, h_inpw)
                if "proj" in debug:
                    p0d = dbg_tensor("proj", [3 * 128, T])
                    n_out[0] += 3
                    n_dbgdma[0] += 3
                    p.g.wait(ve, ve.n)

                    def dump_proj(g):
                        for i, rows in ((0, 128), (1, 128), (2, 64)):
                            g.dma_start(out=p0d[i * 128:i * 128 + rows, :],
                                        in_=pj[i][0:rows, :]).then_inc(s_dbg, 16)
                    p.g.ops.append(dump_proj)
                p.flush("l0a")

            routing_phase("rt0", gatew_b, "l0")

            with ExitStack() as ctx:
                e = ctx.enter_context
                MH = (D // 2) // 128
                hh = e(nc.sbuf_tensor("hh0", [128, MH * T], BF16))
                ma = e(nc.sbuf_tensor("ma", [DS, T], F32))
                mb = e(nc.sbuf_tensor("mb", [DS, T], F32))
                yb = e(nc.sbuf_tensor("yb", [DS, T], BF16))
                asml = e(nc.sbuf_tensor("asml", [DS, 8], F32))
                silu_t = [e(nc.sbuf_tensor(f"silu{i}", [128, CW], BF16)) for i in range(2)]
                H1 = [Bank(e(nc.psum_tensor(f"h1_{i}", [128, CW], F32))) for i in range(2)]
                H3 = [Bank(e(nc.psum_tensor(f"h3_{i}", [128, CW], F32))) for i in range(2)]
                YB = [Bank(e(nc.psum_tensor(f"yb_{i}", [128, CW], F32))) for i in range(2)]
                GB = Bank(e(nc.psum_tensor("gbp", [128, CW], F32)))
                OP = Bank(e(nc.psum_tensor("opp", [128, CW], F32)))

                slotA = pj[0][0:DS, :]
                Be = pj[0][DS:2 * DS, :]
                slotB = pj[1][0:DS, :]
                Ce = pj[1][DS:2 * DS, :]
                slotC = pj[2][0:DS, :]
                delta = slotA


                def run_interleaved_l0(a, b):
                    gens = [a, b]
                    while gens:
                        nxt = []
                        for g in gens:
                            try:
                                next(g)
                                nxt.append(g)
                            except StopIteration:
                                pass
                        gens = nxt

                ve_yb_box = [None]

                def gen_mamba():
                    S, V = p.s, p.v
                    if n_dbgdma[0]:
                        V.wait_dma(s_dbg, 16 * n_dbgdma[0])
                        S.wait_dma(s_dbg, 16 * n_dbgdma[0])
                    S.emit(lambda s: s.activation(ma[:], delta, AF.Abs))
                    S.drain()
                    S.emit(lambda s: s.activation(ma[:], ma[:], AF.Exp, scale=-1.0))
                    S.drain()
                    S.emit(lambda s: s.activation(ma[:], ma[:], AF.Ln, bias=1.0))
                    S.drain()
                    S.emit(lambda s: s.activation(asml[:, 0:1], alog[:], AF.Exp))
                    sc_spl = S.bar(sc)
                    yield

                    V.emit(lambda v: v.tensor_scalar_max(mb[:], delta, 0.0))
                    V.drain()
                    V.wait(sc, sc_spl)
                    V.emit(lambda v: v.tensor_add(mb[:], mb[:], ma[:]))
                    V.drain()
                    V.emit(lambda v: v.tensor_scalar(asml[:, 1:2], asml[:, 0:1], -1.0, None, ALU.mult))
                    V.drain()
                    V.emit(lambda v: v.tensor_scalar(asml[:, 2:3], asml[:, 0:1], EPS_A, None, ALU.is_lt))
                    V.drain()
                    V.emit(lambda v: v.tensor_scalar(asml[:, 3:4], asml[:, 2:3], -1.0, 1.0, ALU.mult, ALU.add))
                    V.drain()
                    V.emit(lambda v: v.tensor_mul(asml[:, 4:5], asml[:, 1:2], asml[:, 3:4]))
                    V.drain()
                    V.emit(lambda v: v.tensor_add(asml[:, 4:5], asml[:, 4:5], asml[:, 2:3]))
                    V.drain()
                    V.emit(lambda v: v.reciprocal(asml[:, 5:6], asml[:, 4:5]))
                    V.drain()
                    V.emit(lambda v: v.tensor_scalar(ma[:], mb[:], asml[:, 1:2], None, ALU.mult))
                    V.drain()
                    V.emit(lambda v: v.tensor_scalar(slotA, ma[:], 1.0 / 6.0, 0.5, ALU.mult, ALU.add))
                    V.drain()
                    V.emit(lambda v: v.tensor_mul(slotA, slotA, ma[:]))
                    V.drain()
                    V.emit(lambda v: v.tensor_scalar_add(slotA, slotA, 1.0))
                    ve_tay = V.bar(ve)
                    yield
                    S.wait(ve, ve_tay)
                    S.emit(lambda s: s.activation(mb[:], ma[:], AF.Exp))
                    sc_abar = S.bar(sc)
                    yield
                    V.wait(sc, sc_abar)
                    V.emit(lambda v: v.tensor_scalar(ma[:], mb[:], -1.0, asml[:, 5:6], ALU.add, ALU.mult))
                    V.drain()
                    V.emit(lambda v: v.tensor_scalar(ma[:], ma[:], asml[:, 3:4], None, ALU.mult))
                    V.drain()
                    V.emit(lambda v: v.tensor_scalar(slotA, slotA, asml[:, 2:3], None, ALU.mult))
                    V.drain()
                    V.emit(lambda v: v.tensor_add(ma[:], ma[:], slotA))
                    V.drain()
                    V.emit(lambda v: v.tensor_copy(slotA, Be))
                    V.drain()
                    V.emit(lambda v: v.tensor_add(slotB, slotA, slotB))
                    V.drain()
                    V.emit(lambda v: v.scalar_tensor_tensor(slotB, slotB, 0.5, ma[:], ALU.mult, ALU.mult))
                    V.drain()
                    V.emit(lambda v: v.tensor_tensor_scan(slotA, mb[:], slotB, 0.0, ALU.mult, ALU.add))
                    V.drain()
                    V.emit(lambda v: v.tensor_copy(mb[:], Ce))
                    V.drain()
                    V.emit(lambda v: v.tensor_add(mb[:], mb[:], slotC))
                    V.drain()
                    V.emit(lambda v: v.scalar_tensor_tensor(mb[:], mb[:], 0.5, slotA, ALU.mult, ALU.mult))
                    V.drain()
                    V.emit(lambda v: v.tensor_copy(yb[:], mb[:]))
                    ve_yb = V.bar(ve)
                    yield
                    if "ystate" in debug:
                        dump_dbg("ystate", yb, [DS, T], (ve, ve_yb), prows=DS)

                    ve_yb_box[0] = ve_yb
                    wait_consts(p.t)
                    p.t.wait(ve, ve_yb)
                    for m in range(KT):
                        for c in range(NCH):
                            OP.acquire(p.t)
                            p.t.emit(lambda t, m=m, c=c: t.matmul(
                                OP.t[:], outpw_b[:, m * 128:(m + 1) * 128],
                                yb[:, c * CW:(c + 1) * CW], start=True, stop=False))
                            p.t.emit(lambda t, m=m, c=c: t.matmul(
                                OP.t[:], outpb_row[0:1, m * 128:(m + 1) * 128],
                                onesrow[0:1, :], start=False, stop=True))
                            mm_op = p.t.bar(mm)
                            p.v.wait(mm, mm_op)
                            p.v.emit(lambda v, m=m, c=c: v.tensor_add(
                                xv(m, c), xv(m, c), OP.t[:]))
                            ve_op = p.v.bar(ve)
                            OP.guard(ve, ve_op)
                            yield

                def gen_moe():
                    NPASS = 10
                    hh_guard = [None]
                    silu_guard = [None, None]
                    for psi in range(NPASS):
                        gated = psi < 8
                        if gated:
                            for c in range(NCH):
                                GB.acquire(p.t)
                                p.t.wait(ve, VE_CONST)
                                p.t.emit(lambda t, psi=psi, c=c: t.matmul(
                                    GB.t[:], ebc[:, psi * 128:(psi + 1) * 128],
                                    gt[0:8, c * CW:(c + 1) * CW], start=True, stop=True))
                                mm_gb = p.t.bar(mm)
                                p.v.wait(mm, mm_gb)
                                p.v.drain()
                                p.v.emit(lambda v, c=c: v.tensor_copy(
                                    gb_sb[:, c * CW:(c + 1) * CW], GB.t[:]))
                                ve_gb = p.v.bar(ve)
                                GB.guard(ve, ve_gb)
                        for fh in range(2):
                            h_w1 = ring.load(p, d_moe1[psi * D:(psi + 1) * D,
                                                       fh * (D // 2):(fh + 1) * (D // 2)],
                                             128, KT, D // 2)
                            h_w3 = ring.load(p, d_moe3[psi * D:(psi + 1) * D,
                                                       fh * (D // 2):(fh + 1) * (D // 2)],
                                             128, KT, D // 2)
                            for m in range(MH):
                                for c in range(NCH):
                                    i = (m * NCH + c) % 2
                                    ring.use(p.t, h_w1)
                                    ring.use(p.t, h_w3)
                                    H1[i].acquire(p.t)
                                    H3[i].acquire(p.t)
                                    for k in range(KT):
                                        p.t.emit(lambda t, k=k, m=m, c=c, i=i, h=h_w1: t.matmul(
                                            H1[i].t[:], ring.lhsT(h, k, m * 128, 128), xnv(k, c),
                                            start=(k == 0), stop=(k == KT - 1)))
                                    for k in range(KT):
                                        p.t.emit(lambda t, k=k, m=m, c=c, i=i, h=h_w3: t.matmul(
                                            H3[i].t[:], ring.lhsT(h, k, m * 128, 128), xnv(k, c),
                                            start=(k == 0), stop=(k == KT - 1)))
                                    mm_h = p.t.bar(mm)
                                    p.s.wait(mm, mm_h)
                                    if silu_guard[i] is not None:
                                        p.s.wait(ve, silu_guard[i])
                                    p.s.emit(lambda s, i=i: s.activation(
                                        silu_t[i][:], H1[i].t[:], AF.Silu))
                                    sc_si = p.s.bar(sc)
                                    H1[i].guard(sc, sc_si)
                                    p.v.wait(sc, sc_si)
                                    if hh_guard[0] is not None:
                                        p.v.wait(mm, hh_guard[0])
                                    hcol = m * T + c * CW
                                    p.v.emit(lambda v, i=i, hcol=hcol: v.tensor_mul(
                                        hh[:, hcol:hcol + CW], silu_t[i][:], H3[i].t[:]))
                                    ve_hh = p.v.bar(ve)
                                    H3[i].guard(ve, ve_hh)
                                    silu_guard[i] = ve_hh
                                    yield
                                    if gated:
                                        p.v.drain()
                                        p.v.emit(lambda v, c=c, hcol=hcol: v.tensor_mul(
                                            hh[:, hcol:hcol + CW], hh[:, hcol:hcol + CW],
                                            gb_sb[:, c * CW:(c + 1) * CW]))
                                        ve_hh = p.v.bar(ve)
                            ve_hh_all = ve.n
                            ring.release(p.t, h_w1)
                            ring.release(p.t, h_w3)
                            NSL = max(1, D // CW)
                            CW2 = min(CW, D)
                            r0 = psi * D + fh * (D // 2)
                            for sl in range(NSL):
                                h_w2 = ring.load(p, d_moe2[r0:r0 + D // 2,
                                                           sl * CW2:(sl + 1) * CW2],
                                                 128, MH, CW2)
                                for ml in range(CW2 // 128):
                                    for c in range(NCH):
                                        i = (ml * NCH + c) % 2
                                        mglob = sl * (CW2 // 128) + ml
                                        ring.use(p.t, h_w2)
                                        p.t.wait(ve, ve_hh_all)
                                        YB[i].acquire(p.t)
                                        for k in range(MH):
                                            p.t.emit(lambda t, k=k, ml=ml, c=c, i=i, h=h_w2: t.matmul(
                                                YB[i].t[:], ring.lhsT(h, k, ml * 128, 128),
                                                hh[:, k * T + c * CW: k * T + (c + 1) * CW],
                                                start=(k == 0), stop=(k == MH - 1)))
                                        mm_y = p.t.bar(mm)
                                        p.v.wait(mm, mm_y)
                                        p.v.emit(lambda v, i=i, mglob=mglob, c=c: v.tensor_add(
                                            xv(mglob, c), xv(mglob, c), YB[i].t[:]))
                                        ve_y = p.v.bar(ve)
                                        YB[i].guard(ve, ve_y)
                                        yield
                                ring.release(p.t, h_w2)
                            hh_guard[0] = mm.n

                run_interleaved_l0(gen_mamba(), gen_moe())
                if "x1" in debug:
                    dump_dbg("x1", x_sb, [D, T], (ve, ve.n))
                p.flush("l0b")

        ln_phase("ln1")
        routing_phase("rt1", routerw_b, "l1")

        D2 = D // 2
        HHALF = NH // 2
        PPH = HHALF // 2
        BK = 64
        NBLK = T // 64
        BPC = CW // BK
        NQH = 4
        VG = NQH * 64
        NQ = NTT * NBLK
        with ExitStack() as ctx:
            e = ctx.enter_context
            dbg_fence()
            v_t = [e(nc.sbuf_tensor(f"v_t{i}", [128, NTT * VG], BF16)) for i in range(2)]
            o_pair = e(nc.sbuf_tensor("o_pair", [128, T], BF16))
            q2 = [e(nc.sbuf_tensor(f"q2_{i}", [128, T], BF16)) for i in range(2)]
            k2 = [e(nc.sbuf_tensor(f"k2_{i}", [128, T], BF16)) for i in range(2)]
            pt_e = [e(nc.sbuf_tensor(f"pt_e{i}", [128, CW], BF16)) for i in range(3)]
            pt_o = [e(nc.sbuf_tensor(f"pt_o{i}", [128, CW], BF16)) for i in range(3)]
            es = [e(nc.sbuf_tensor(f"es{i}", [128, CW], F32)) for i in range(2)]
            mu_sb = [e(nc.sbuf_tensor("mu0", [128, NQ], F32))] * 2
            sig_sb = [e(nc.sbuf_tensor("sg0", [128, NQ], F32))] * 2
            bw = [e(nc.sbuf_tensor(f"bw{i}", [128, NQ], F32)) for i in range(4)]
            bw.append(sig_sb[0])
            zt = e(nc.sbuf_tensor("zt", [128, NQ], F32))
            btr2 = [e(nc.sbuf_tensor(f"btr2_{i}", [128, T], BF16)) for i in range(2)]
            B = [Bank(e(nc.psum_tensor(f"atb{i}", [128, CW], F32))) for i in range(8)]
            OE, OO = B[4], B[5]

            p.v.emit(lambda v: v.memset(zt[:], 0.0))
            p.v.bar(ve)

            guards = dict(vt=[None, None], opair=None, pt=[None] * 3, es=[None, None],
                          q2=[None, None], btr=[None, None])

            def gen_prep(u):
                pg = u
                half = u // PPH
                pb = u % 2
                h_wq = ring.load(p, d_wq[:, pg * 128:(pg + 1) * 128], 128, KT, 128)
                h_wk = ring.load(p, d_wk[:, pg * 128:(pg + 1) * 128], 128, KT, 128)
                for c in range(NCH):
                    QB = B[6 + (c % 2)]
                    ring.use(p.t, h_wq)
                    QB.acquire(p.t)
                    if guards["q2"][pb] is not None:
                        p.t.wait(mm, guards["q2"][pb])
                    for k in range(KT):
                        p.t.emit(lambda t, k=k, c=c, QB=QB, h=h_wq: t.matmul(
                            QB.t[:], ring.lhsT(h, k, 0, 128), xnv(k, c),
                            start=(k == 0), stop=(k == KT - 1)))
                    mm_q = p.t.bar(mm)
                    p.s.wait(mm, mm_q)
                    p.s.emit(lambda s, c=c, QB=QB, pb=pb: s.activation(
                        q2[pb][:, c * CW:(c + 1) * CW], QB.t[:], AF.Copy, scale=fc(0.125)))
                    sc_q = p.s.bar(sc)
                    QB.guard(sc, sc_q)
                    yield
                    QB2 = B[6 + ((c + 1) % 2)]
                    ring.use(p.t, h_wk)
                    QB2.acquire(p.t)
                    for k in range(KT):
                        p.t.emit(lambda t, k=k, c=c, QB2=QB2, h=h_wk: t.matmul(
                            QB2.t[:], ring.lhsT(h, k, 0, 128), xnv(k, c),
                            start=(k == 0), stop=(k == KT - 1)))
                    mm_k = p.t.bar(mm)
                    p.s.wait(mm, mm_k)
                    p.s.emit(lambda s, c=c, QB2=QB2, pb=pb: s.activation(
                        k2[pb][:, c * CW:(c + 1) * CW], QB2.t[:], AF.Copy))
                    sc_k = p.s.bar(sc)
                    QB2.guard(sc, sc_k)
                    yield
                ring.release(p.t, h_wq)
                ring.release(p.t, h_wk)
                sc_qk = sc.n
                sti = 0
                for par in range(2):
                    rb0 = 64 * par
                    tp = (0, 0) if par == 0 else (64, 0)
                    for qt in range(NTT):
                        for kvc in range(NCH):
                            SB_ = (B[3], B[6], B[7])[sti % 3]
                            sti += 1
                            p.t.wait(sc, sc_qk)
                            SB_.acquire(p.t)
                            p.t.emit(lambda t, rb0=rb0, qt=qt, kvc=kvc, SB_=SB_, tp=tp, pb=pb: t.matmul(
                                SB_.t[:], q2[pb][rb0:rb0 + 64, qt * 128:(qt + 1) * 128],
                                k2[pb][rb0:rb0 + 64, kvc * CW:(kvc + 1) * CW],
                                start=True, stop=True, tile_position=tp))
                            mm_st = p.t.bar(mm)
                            p.v.wait(mm, mm_st)
                            col = qt * NBLK + kvc * BPC
                            p.v.emit(lambda v, par=par, col=col, SB_=SB_: v.tensor_reduce(
                                bass.AP(mu_sb[par], col, [[NQ, 128], [1, BPC], [1, 1]]),
                                SB_.t[:].rearrange("p (b k) -> p b k", k=BK),
                                axis=AX.X, op=ALU.max))
                            ve_mu = p.v.bar(ve)
                            esb = sti % 2
                            p.s.wait(mm, mm_st)
                            p.s.wait(ve, ve_mu)
                            if guards["es"][esb] is not None:
                                p.s.wait(ve, guards["es"][esb])
                            p.s.emit(lambda s, esb=esb, SB_=SB_: s.activation(
                                es[esb][:], SB_.t[:], AF.Exp))
                            sc_es = p.s.bar(sc)
                            SB_.guard(sc, sc_es)
                            p.v.wait(sc, sc_es)
                            p.v.emit(lambda v, par=par, col=col, esb=esb: v.tensor_reduce(
                                bass.AP(sig_sb[par], col, [[NQ, 128], [1, BPC], [1, 1]]),
                                es[esb][:].rearrange("p (b k) -> p b k", k=BK),
                                axis=AX.X, op=ALU.add))
                            ve_sg = p.v.bar(ve)
                            guards["es"][esb] = ve_sg
                            yield
                    V_, S_ = p.v, p.s
                    rb = 64 * par

                    def g3(tsb):
                        return tsb[:, 0:NQ].rearrange("p (q b) -> p q b", b=NBLK)
                    V_.drain()
                    for qt in range(NTT):
                        V_.emit(lambda v, par=par, qt=qt: v.tensor_tensor_scan(
                            bw[0][:, qt * NBLK:(qt + 1) * NBLK],
                            mu_sb[par][:, qt * NBLK:(qt + 1) * NBLK],
                            mu_sb[par][:, qt * NBLK:(qt + 1) * NBLK],
                            -1e30, ALU.max, ALU.max))
                    V_.drain()
                    V_.emit(lambda v: v.memset(
                        bass.AP(bw[1], 0, [[NQ, 128], [NBLK, NTT], [1, 1]]), -1e30))
                    V_.drain()
                    V_.emit(lambda v: v.tensor_copy(
                        bass.AP(bw[1], 1, [[NQ, 128], [NBLK, NTT], [1, NBLK - 1]]),
                        bass.AP(bw[0], 0, [[NQ, 128], [NBLK, NTT], [1, NBLK - 1]])))
                    V_.drain()
                    V_.emit(lambda v: v.tensor_sub(bw[1][:, 0:NQ], bw[1][:, 0:NQ], bw[0][:, 0:NQ]))
                    ve_d = V_.bar(ve)
                    yield
                    S_.wait(ve, ve_d)
                    S_.emit(lambda s: s.activation(bw[2][:, 0:NQ], bw[1][:, 0:NQ], AF.Exp))
                    S_.drain()
                    S_.emit(lambda s: s.activation(bw[3][:, 0:NQ], bw[0][:, 0:NQ], AF.Exp, scale=-1.0))
                    sc_a = S_.bar(sc)
                    V_.wait(sc, sc_a)
                    V_.emit(lambda v, par=par: v.tensor_mul(
                        bw[3][:, 0:NQ], sig_sb[par][:, 0:NQ], bw[3][:, 0:NQ]))
                    V_.drain()
                    for qt in range(NTT):
                        V_.emit(lambda v, qt=qt: v.tensor_tensor_scan(
                            bw[4][:, qt * NBLK:(qt + 1) * NBLK],
                            bw[2][:, qt * NBLK:(qt + 1) * NBLK],
                            bw[3][:, qt * NBLK:(qt + 1) * NBLK],
                            0.0, ALU.mult, ALU.add))
                    ve_l = V_.bar(ve)
                    yield
                    S_.wait(ve, ve_l)
                    S_.emit(lambda s: s.activation(bw[2][:, 0:NQ], bw[4][:, 0:NQ], AF.Ln))
                    sc_lnl = S_.bar(sc)
                    V_.wait(sc, sc_lnl)
                    V_.emit(lambda v: v.tensor_sub(bw[1][:, 0:NQ], bw[1][:, 0:NQ], bw[2][:, 0:NQ]))
                    V_.drain()
                    V_.emit(lambda v: v.memset(
                        bass.AP(bw[1], 0, [[NQ, 128], [NBLK, NTT], [1, 1]]), 0.0))
                    V_.drain()
                    for qt in range(NTT):
                        V_.emit(lambda v, qt=qt: v.tensor_tensor_scan(
                            bw[3][:, qt * NBLK:(qt + 1) * NBLK],
                            bw[1][:, qt * NBLK:(qt + 1) * NBLK],
                            zt[:, qt * NBLK:(qt + 1) * NBLK],
                            0.0, ALU.add, ALU.add))
                    V_.drain()
                    V_.emit(lambda v: v.tensor_add(bw[0][:, 0:NQ], bw[0][:, 0:NQ], bw[2][:, 0:NQ]))
                    V_.drain()
                    V_.emit(lambda v: v.tensor_add(bw[0][:, 0:NQ], bw[0][:, 0:NQ], bw[3][:, 0:NQ]))
                    V_.drain()
                    V_.emit(lambda v: v.tensor_tensor(
                        g3(bw[0]), g3(bw[0]),
                        bass.AP(bw[3], NBLK - 1, [[NQ, 128], [NBLK, NTT], [0, NBLK]]),
                        ALU.subtract))
                    ve_b = V_.bar(ve)
                    yield
                    for qt in range(NTT):
                        TRP = B[6 + (qt % 2)]
                        p.t.wait(ve, ve_b)
                        TRP.acquire(p.t)
                        p.t.emit(lambda t, qt=qt, TRP=TRP: t.transpose(
                            TRP.t[0:NBLK, 0:128], bw[0][:, qt * NBLK:(qt + 1) * NBLK],
                            identf[:]))
                        mm_tr = p.t.bar(mm)
                        p.v.wait(mm, mm_tr)
                        if guards["btr"][pb] is not None:
                            p.v.wait(mm, guards["btr"][pb])
                        p.v.drain()
                        p.v.emit(lambda v, pb=pb, rb=rb, qt=qt, TRP=TRP: v.tensor_copy(
                            btr2[pb][rb:rb + NBLK, qt * 128:(qt + 1) * 128],
                            TRP.t[0:NBLK, 0:128]))
                        ve_bt = p.v.bar(ve)
                        TRP.guard(ve, ve_bt)
                        if qt % 4 == 3:
                            yield
                    p.v.wait(mm, mm.n)

            def gen_vprod(vg):
                vb_ = vg % 2
                h_wv = ring.load(p, d_wv[:, vg * VG:(vg + 1) * VG], 128, KT, VG)
                if guards["vt"][vb_] is not None:
                    p.v.wait(mm, guards["vt"][vb_])
                for tt in range(NTT):
                    VB = B[tt % 2]
                    ring.use(p.t, h_wv)
                    VB.acquire(p.t)
                    for k in range(KT):
                        p.t.emit(lambda t, k=k, tt=tt, VB=VB, h=h_wv: t.matmul(
                            VB.t[:, 0:VG], xn[:, k * T + tt * 128: k * T + (tt + 1) * 128],
                            ring.lhsT(h, k, 0, VG), start=(k == 0), stop=(k == KT - 1)))
                    mm_v = p.t.bar(mm)
                    p.v.wait(mm, mm_v)
                    p.v.emit(lambda v, tt=tt, VB=VB, vb_=vb_: v.tensor_copy(
                        v_t[vb_][:, tt * VG:(tt + 1) * VG], VB.t[:, 0:VG]))
                    ve_vt = p.v.bar(ve)
                    VB.guard(ve, ve_vt)
                    if tt % 2 == 1:
                        yield
                ring.release(p.t, h_wv)

            def gen_st(u):
                pg = u
                pb = u % 2
                pp_in_vg = u % 2
                ve_beta = ve.n

                vb_ = (u // 2) % 2

                def emit_o(pend):
                    kt, ip, sc_p = pend
                    p.t.wait(sc, sc_p)
                    p.t.emit(lambda t, kt=kt, ip=ip: t.matmul(
                        OE.t[0:64, :], v_t[vb_][:, kt * VG + (2 * pp_in_vg) * 64: kt * VG + (2 * pp_in_vg) * 64 + 64],
                        pt_e[ip][:], start=(kt == 0), stop=(kt == NTT - 1)))
                    p.t.emit(lambda t, kt=kt, ip=ip: t.matmul(
                        OO.t[0:64, :], v_t[vb_][:, kt * VG + (2 * pp_in_vg + 1) * 64: kt * VG + (2 * pp_in_vg + 1) * 64 + 64],
                        pt_o[ip][:], start=(kt == 0), stop=(kt == NTT - 1)))
                    guards["pt"][ip] = p.t.bar(mm)

                for qc in range(NCH):
                    OE.acquire(p.t)
                    OO.acquire(p.t)
                    pend = None
                    for kt in range(NTT):
                        i = kt % 2
                        ip = kt % 3
                        SE_ = B[i]
                        SO_ = B[2]
                        p.t.wait(ve, ve_beta)
                        SE_.acquire(p.t)
                        SO_.acquire(p.t)
                        p.t.emit(lambda t, kt=kt, qc=qc, SE_=SE_, pb=pb: t.matmul(
                            SE_.t[:], k2[pb][0:64, kt * 128:(kt + 1) * 128],
                            q2[pb][0:64, qc * CW:(qc + 1) * CW],
                            start=True, stop=False, tile_position=(0, 0)))
                        p.t.emit(lambda t, kt=kt, qc=qc, SO_=SO_, pb=pb: t.matmul(
                            SO_.t[:], k2[pb][64:128, kt * 128:(kt + 1) * 128],
                            q2[pb][64:128, qc * CW:(qc + 1) * CW],
                            start=True, stop=False, tile_position=(64, 0)))
                        p.t.emit(lambda t, kt=kt, qc=qc, SE_=SE_, pb=pb: t.matmul(
                            SE_.t[:], negE[0:NBLK, kt * 128:(kt + 1) * 128],
                            btr2[pb][0:NBLK, qc * CW:(qc + 1) * CW],
                            start=False, stop=True))
                        p.t.emit(lambda t, kt=kt, qc=qc, SO_=SO_, pb=pb: t.matmul(
                            SO_.t[:], negE[64:64 + NBLK, kt * 128:(kt + 1) * 128],
                            btr2[pb][64:64 + NBLK, qc * CW:(qc + 1) * CW],
                            start=False, stop=True))
                        mm_s = p.t.bar(mm)
                        p.s.wait(mm, mm_s)
                        if guards["pt"][ip] is not None:
                            p.s.wait(mm, guards["pt"][ip])
                        p.s.emit(lambda s, ip=ip, SE_=SE_: s.activation(pt_e[ip][:], SE_.t[:], AF.Exp))
                        p.s.emit(lambda s, ip=ip, SO_=SO_: s.activation(pt_o[ip][:], SO_.t[:], AF.Exp))
                        sc_p = p.s.bar(sc)
                        SE_.guard(sc, sc_p)
                        SO_.guard(sc, sc_p)
                        if pend is not None:
                            emit_o(pend)
                        pend = (kt, ip, sc_p)
                        if kt % 2 == 1:
                            yield
                    emit_o(pend)
                    mm_o = mm.n
                    p.s.wait(mm, mm_o)
                    if guards["opair"] is not None:
                        p.s.wait(mm, guards["opair"])
                    p.s.emit(lambda s, qc=qc: s.activation(
                        o_pair[0:64, qc * CW:(qc + 1) * CW], OE.t[0:64, :], AF.Copy))
                    p.s.emit(lambda s, qc=qc: s.activation(
                        o_pair[64:128, qc * CW:(qc + 1) * CW], OO.t[0:64, :], AF.Copy))
                    sc_on = p.s.bar(sc)
                    OE.guard(sc, sc_on)
                    OO.guard(sc, sc_on)
                    yield
                guards["q2"][pb] = mm.n
                guards["btr"][pb] = mm.n
                sc_opair = sc.n
                h_wo = ring.load(p, d_wo[pg * 128:(pg + 1) * 128, :], 128, 1, D)
                for m in range(KT):
                    for c in range(NCH):
                        WB_ = B[(m * NCH + c) % 2]
                        ring.use(p.t, h_wo)
                        p.t.wait(sc, sc_opair)
                        WB_.acquire(p.t)
                        p.t.emit(lambda t, m=m, c=c, h=h_wo, WB_=WB_: t.matmul(
                            WB_.t[:], ring.lhsT(h, 0, m * 128, 128),
                            o_pair[:, c * CW:(c + 1) * CW], start=True, stop=True))
                        mm_wo = p.t.bar(mm)
                        p.v.wait(mm, mm_wo)
                        p.v.emit(lambda v, m=m, c=c, WB_=WB_: v.tensor_add(
                            xv(m, c), xv(m, c), WB_.t[:]))
                        ve_wo = p.v.bar(ve)
                        WB_.guard(ve, ve_wo)
                    if m % 2 == 1:
                        yield
                ring.release(p.t, h_wo)
                guards["opair"] = mm.n
                guards["vt"][vb_] = mm.n

            def run_interleaved(a, b):
                gens = [g for g in (a, b) if g is not None]
                while gens:
                    nxt = []
                    for g in gens:
                        try:
                            next(g)
                            nxt.append(g)
                        except StopIteration:
                            pass
                    gens = nxt

            NPTOT = NH // 2
            for u in range(NPTOT + 1):
                pre = None
                if u < NPTOT:
                    if u % 2 == 0:
                        def pre_gen(u=u):
                            yield from gen_vprod(u // 2)
                            yield from gen_prep(u)
                        pre = pre_gen()
                    else:
                        pre = gen_prep(u)
                stg = gen_st(u - 1) if u > 0 else None
                run_interleaved(pre, stg)
            if "x1a" in debug:
                dump_dbg("x1a", x_sb, [D, T], (ve, ve.n))
            p.flush("attn")

        with ExitStack() as ctx:
            e = ctx.enter_context
            dbg_fence()
            MQ = D // 128
            hh1 = e(nc.sbuf_tensor("hh1", [128, MQ * T], BF16))
            H1 = [Bank(e(nc.psum_tensor(f"f_h1_{i}", [128, CW], F32))) for i in range(2)]
            YB = [Bank(e(nc.psum_tensor(f"f_yb_{i}", [128, CW], F32))) for i in range(2)]
            GB = Bank(e(nc.psum_tensor("f_gb", [128, CW], F32)))
            SLW = min(CW, D)
            NSL = D // SLW
            hh_guard = [None]
            b2_guard = [None, None]
            nb2 = [0]
            for ei in range(NE):
                bi = ei % 2
                if b2_guard[bi] is not None:
                    p.g.wait(s_b2f, b2_guard[bi])
                nb2[0] += 1

                def ldb2(g, ei=ei, bi=bi):
                    g.dma_start(out=b2cur[bi][:],
                                in_=d_b2sel[ei * NE:(ei + 1) * NE, :]).then_inc(s_b2, 16)
                p.g.ops.append(ldb2)
                nb2_here = nb2[0]
                for c in range(NCH):
                    GB.acquire(p.t)
                    p.t.wait(ve, VE_CONST)
                    p.t.emit(lambda t, ei=ei, c=c: t.matmul(
                        GB.t[:], ebc[:, ei * 128:(ei + 1) * 128],
                        gt[0:8, c * CW:(c + 1) * CW], start=True, stop=True))
                    mm_gb = p.t.bar(mm)
                    p.v.wait(mm, mm_gb)
                    p.v.drain()
                    p.v.emit(lambda v, c=c: v.tensor_copy(
                        gb_sb[:, c * CW:(c + 1) * CW], GB.t[:]))
                    ve_gb = p.v.bar(ve)
                    GB.guard(ve, ve_gb)
                for fq in range(4):
                    for sl in range(NSL):
                        h_w1 = ring.load(p, d_w1[ei * D:(ei + 1) * D,
                                                 fq * D + sl * SLW: fq * D + (sl + 1) * SLW],
                                         128, KT, SLW)
                        for ml in range(SLW // 128):
                            m = sl * (SLW // 128) + ml
                            for c in range(NCH):
                                i = (m * NCH + c) % 2
                                ring.use(p.t, h_w1)
                                H1[i].acquire(p.t)
                                for k in range(KT):
                                    p.t.emit(lambda t, k=k, ml=ml, c=c, i=i, h=h_w1: t.matmul(
                                        H1[i].t[:], ring.lhsT(h, k, ml * 128, 128), xnv(k, c),
                                        start=(k == 0), stop=(k == KT - 1)))
                                mm_h = p.t.bar(mm)
                                p.s.wait(mm, mm_h)
                                if hh_guard[0] is not None:
                                    p.s.wait(mm, hh_guard[0])
                                bcol = ei * (F1 // 128) + fq * MQ + m
                                p.s.emit(lambda s, i=i, m=m, c=c, bcol=bcol: s.activation(
                                    hh1[:, m * T + c * CW: m * T + (c + 1) * CW],
                                    H1[i].t[:], AF.Gelu, bias=b1_sb[:, bcol:bcol + 1]))
                                sc_h = p.s.bar(sc)
                                H1[i].guard(sc, sc_h)
                                p.v.wait(sc, sc_h)
                                p.v.emit(lambda v, m=m, c=c: v.tensor_mul(
                                    hh1[:, m * T + c * CW: m * T + (c + 1) * CW],
                                    hh1[:, m * T + c * CW: m * T + (c + 1) * CW],
                                    gb_sb[:, c * CW:(c + 1) * CW]))
                                p.v.drain()
                        ring.release(p.t, h_w1)
                    ve_hh = p.v.bar(ve)
                    for sl in range(NSL):
                        h_w2 = ring.load(p, d_w2[ei * F1 + fq * D: ei * F1 + (fq + 1) * D,
                                                 sl * SLW:(sl + 1) * SLW],
                                         128, KT, SLW)
                        for ml in range(SLW // 128):
                            mg = sl * (SLW // 128) + ml
                            for c in range(NCH):
                                i = (mg * NCH + c) % 2
                                ring.use(p.t, h_w2)
                                p.t.wait(ve, ve_hh)
                                YB[i].acquire(p.t)
                                for k in range(KT):
                                    p.t.emit(lambda t, k=k, ml=ml, c=c, i=i, h=h_w2, fq=fq: t.matmul(
                                        YB[i].t[:], ring.lhsT(h, k, ml * 128, 128),
                                        hh1[:, k * T + c * CW: k * T + (c + 1) * CW],
                                        start=(k == 0), stop=(fq != 0 and k == KT - 1)))
                                if fq == 0:
                                    p.t.wait_dma(s_b2, 16 * nb2_here)
                                    p.t.emit(lambda t, mg=mg, c=c, i=i, bi=bi: t.matmul(
                                        YB[i].t[:], b2cur[bi][0:8, mg * 128:(mg + 1) * 128],
                                        gt[0:8, c * CW:(c + 1) * CW],
                                        start=False, stop=True))
                                mm_y = p.t.bar(mm)
                                p.v.wait(mm, mm_y)
                                p.v.emit(lambda v, mg=mg, c=c, i=i: v.tensor_add(
                                    xv(mg, c), xv(mg, c), YB[i].t[:]))
                                ve_y = p.v.bar(ve)
                                YB[i].guard(ve, ve_y)
                        ring.release(p.t, h_w2)
                    hh_guard[0] = mm.n
                b2_guard[bi] = p.t.bar(s_b2f)
            p.flush("l1ffn")

        def _fin(s):
            s.dma_start(out=d_y.rearrange("(k p) t -> p k t", p=128),
                        in_=x_sb[:].rearrange("p (k t) -> p k t", k=KT)).then_inc(s_out, 16)
            s.wait_ge(s_out, 16)
            if n_out[0] > 1:
                s.wait_ge(s_dbg, 16 * (n_out[0] - 1))
        p.sy.ops.append(_fin)
        p.flush("fin")

    return nc



def _prep_weights(i, cfg):
    D = cfg["D"]
    F1 = 4 * D
    c = np.ascontiguousarray
    PERM = np.concatenate([np.arange(DS),
                           DS + 2 * np.arange(DS), DS + 1 + 2 * np.arange(DS),
                           3 * DS + 2 * np.arange(DS), 3 * DS + 1 + 2 * np.arange(DS)])
    inpw = c(i["L0_attn_in_proj_w"][PERM].T)
    inpb = np.zeros((3, 128), np.float32)
    inpb.reshape(-1)[:5 * DS] = i["L0_attn_in_proj_b"][PERM]
    inpb = c(inpb.T)
    convw = c(i["L0_attn_conv_w"][:, 0, :].reshape(D // 128, 128, DCONV)
              .transpose(1, 0, 2).reshape(128, -1))
    convb = c(i["L0_attn_conv_b"].reshape(D // 128, 128).T)
    alog = c(i["L0_attn_A_log"].reshape(DS, 1))
    outpw = c(i["L0_attn_out_proj_w"].T)
    outpb = c(i["L0_attn_out_proj_b"].reshape(1, D))
    gatew = c(i["L0_ffn_gate_w"].T)
    gbias_tok = c(np.broadcast_to(i["L0_ffn_gate_bias"][None, :], (128, NE)).astype(np.float32))
    routerw = c(i["L1_ffn_router_w"].T)
    w1t = i["L0_ffn_w1"].transpose(0, 2, 1)
    w3t = i["L0_ffn_w3"].transpose(0, 2, 1)
    w2t = i["L0_ffn_w2"].transpose(0, 2, 1)
    sw1t = i["L0_ffn_sw1"].T
    sw3t = i["L0_ffn_sw3"].T
    sw2t = i["L0_ffn_sw2"].T
    moe1 = c(np.concatenate([w1t.reshape(8 * D, D), sw1t[:, :D], sw1t[:, D:]], axis=0))
    moe3 = c(np.concatenate([w3t.reshape(8 * D, D), sw3t[:, :D], sw3t[:, D:]], axis=0))
    moe2 = c(np.concatenate([w2t.reshape(8 * D, D), sw2t[:D, :], sw2t[D:, :]], axis=0))
    l1w1 = c(i["L1_ffn_w1"].transpose(0, 2, 1).reshape(NE * D, F1))
    b1 = c(i["L1_ffn_b1"].reshape(NE, F1 // 128, 128).transpose(2, 0, 1).reshape(128, -1))
    l1w2 = c(i["L1_ffn_w2"].transpose(0, 2, 1).reshape(NE * F1, D))
    b2sel = np.zeros((NE, NE * D), np.float32)
    for e in range(NE):
        b2sel[e, e * D:(e + 1) * D] = i["L1_ffn_b2"][e]
    b2sel = b2sel.reshape(NE * NE, D)
    return dict(convw=convw, convb=convb, inpw=inpw, inpb=inpb, alog=alog,
                outpw=outpw, outpb=outpb, gatew=gatew, gbias_tok=gbias_tok,
                routerw=routerw, moe1=moe1, moe3=moe3, moe2=moe2,
                wq=c(i["L1_attn_Wq"]), wk=c(i["L1_attn_Wk"]),
                wv=c(i["L1_attn_Wv"]), wo=c(i["L1_attn_Wo"]),
                l1w1=l1w1, l1b1=b1, l1w2=l1w2, b2sel=b2sel)


def kernel(**inputs):
    cfg = CFG_FULL
    x = inputs["L0_attn_x"]
    B = x.shape[0]
    wmap = _prep_weights(inputs, cfg)
    in_maps = []
    for b in range(B):
        m = dict(wmap)
        m["xt"] = np.ascontiguousarray(np.asarray(x[b]).T)
        in_maps.append(m)
    nc = build(cfg)
    res = run_bass_kernel_spmd(nc, in_maps, core_ids=list(range(B)))
    out = np.stack([res.results[b]["yt"].T for b in range(B)], axis=0)
    return np.ascontiguousarray(out.astype(np.float32))


# version 20 (final)
# speedup vs baseline: 1.1339x; 1.1339x over previous
import numpy as np
from contextlib import ExitStack

import concourse.bass as bass
import concourse.mybir as mybir
from concourse.bass_utils import run_bass_kernel_spmd

F32 = mybir.dt.float32
BF16 = mybir.dt.bfloat16
AF = mybir.ActivationFunctionType
ALU = mybir.AluOpType
AX = mybir.AxisListType

CFG_FULL = dict(D=1024, T=2048, debug=())
CFG_SMALL = dict(D=256, T=512, debug=())

DS = 64
DCONV = 4
NE = 8
EPS_LN = 1e-5
EPS_A = 1e-4
ENAMES = ("t", "v", "s", "sy", "g")


class Ctr:

    def __init__(self, sem):
        self.sem = sem
        self.n = 0


class Stream:

    def __init__(self, prog, ename):
        self.prog = prog
        self.ename = ename
        self.ops = []
        self.last_op = None

    def emit(self, fn, inc=None):
        val = None
        incs = []
        if inc is not None:
            inc.n += 1
            val = inc.n
            incs.append(inc.sem)
        idx = len(self.ops)
        self.ops.append(self._wrap(fn, incs))
        self.last_op = (idx, fn, incs)
        return val

    @staticmethod
    def _wrap(fn, incs):
        if not incs:
            def run(eng, fn=fn):
                fn(eng)
        else:
            def run(eng, fn=fn, incs=tuple(incs)):
                inst = fn(eng)
                for sem in incs:
                    inst.then_inc(sem, 1)
        return run

    def attach_inc(self, sem):
        if (self.last_op is not None and self.last_op[0] == len(self.ops) - 1
                and len(self.last_op[2]) < 1):
            idx, fn, incs = self.last_op
            incs.append(sem)
            self.ops[idx] = self._wrap(fn, incs)
        else:
            fn = lambda eng: eng.drain()
            incs = [sem]
            self.ops.append(self._wrap(fn, incs))
            self.last_op = (len(self.ops) - 1, fn, incs)

    def wait(self, ctr, val):
        if val is None or val <= 0:
            return
        key = (self.ename, id(ctr))
        floors = self.prog.floors
        if floors.get(key, 0) >= val:
            return
        floors[key] = val
        sem = ctr.sem if isinstance(ctr, Ctr) else ctr
        self.ops.append(lambda eng, sem=sem, val=val: eng.wait_ge(sem, val))
        self.last_op = None

    def wait_dma(self, sem, val):
        if val <= 0:
            return
        key = (self.ename, id(sem))
        if self.prog.floors.get(key, 0) >= val:
            return
        self.prog.floors[key] = val
        self.ops.append(lambda eng, sem=sem, val=val: eng.wait_ge(sem, val))
        self.last_op = None

    def drain(self):
        fn = lambda eng: eng.drain()
        self.ops.append(self._wrap(fn, []))
        self.last_op = (len(self.ops) - 1, fn, [])

    def bar(self, ctr):
        ctr.n += 1
        self.attach_inc(ctr.sem)
        return ctr.n


class Prog:
    def __init__(self, nc):
        self.nc = nc
        self.floors = {}
        self._new_streams()

    def _new_streams(self):
        for en in ENAMES:
            setattr(self, en, Stream(self, en))

    def flush(self, name):
        streams = {en: getattr(self, en) for en in ENAMES}
        with self.nc.Block(name, no_gpsimd_drain=True) as blk:
            for en, dec in (("t", blk.tensor), ("v", blk.vector), ("s", blk.scalar),
                            ("sy", blk.sync), ("g", blk.gpsimd)):
                ops = streams[en].ops
                if ops:
                    dec(lambda eng, ops=ops: [f(eng) for f in ops])
        self._new_streams()


class Bank:

    def __init__(self, t):
        self.t = t
        self.guards = []

    def acquire(self, st):
        for ctr, val in self.guards:
            st.wait(ctr, val)
        self.guards = []

    def guard(self, ctr, val=None):
        self.guards.append((ctr, val if val is not None else ctr.n))


class Ring:

    def __init__(self, nc, ctx, nslots, slot_elems):
        self.nslots = nslots
        self.slot_elems = slot_elems
        self.slots = [ctx.enter_context(nc.sbuf_tensor(f"ring{i}", [128, slot_elems], BF16))
                      for i in range(nslots)]
        self.ld = [ctx.enter_context(nc.semaphore(f"rld{i}")) for i in range(nslots)]
        self.fr = [ctx.enter_context(nc.semaphore(f"rfr{i}")) for i in range(nslots)]
        self.nload = 0
        self.nfree = [0] * nslots

    def load(self, p, src2d, prows, kc, cw):
        i = self.nload
        self.nload += 1
        s = i % self.nslots
        gen = i // self.nslots
        assert kc * cw <= self.slot_elems
        if gen > 0:
            p.g.wait_dma(self.fr[s], gen)
        src = src2d.rearrange("(k p) m -> p k m", p=prows)
        slot = self.slots[s]

        def fn(eng, src=src, slot=slot, prows=prows, kc=kc, cw=cw, sem=self.ld[s]):
            dst = slot[0:prows, 0:kc * cw].rearrange("p (k m) -> p k m", k=kc)
            eng.dma_start(out=dst, in_=src).then_inc(sem, 16)
        p.g.ops.append(fn)
        return dict(s=s, gen=gen, kc=kc, cw=cw, prows=prows)

    def slot(self, h):
        return self.slots[h["s"]]

    def use(self, st, h):
        st.wait_dma(self.ld[h["s"]], 16 * (h["gen"] + 1))

    def lhsT(self, h, k, m0, mw):
        return self.slots[h["s"]][0:h["prows"], k * h["cw"] + m0: k * h["cw"] + m0 + mw]

    def release(self, st, h):
        self.nfree[h["s"]] += 1

        st.attach_inc(self.fr[h["s"]])


def build(cfg):
    D = cfg["D"]
    T = cfg["T"]
    KT = D // 128
    CW = min(512, T)
    NCH = T // CW
    NTT = T // 128
    NH = D // 64
    NPAIR = NH // 2
    F1 = 4 * D
    PROJ = 5 * DS
    debug = set(cfg.get("debug", ()))
    stop_after = cfg.get("stop_after", "")

    nc = bass.Bass("TRN2", target_bir_lowering=False)

    d_x = nc.dram_tensor("xt", [D, T], F32, kind="ExternalInput").ap()
    d_y = nc.dram_tensor("yt", [D, T], F32, kind="ExternalOutput").ap()
    d_convw = nc.dram_tensor("convw", [128, KT * DCONV], F32, kind="ExternalInput").ap()
    d_convb = nc.dram_tensor("convb", [128, KT], F32, kind="ExternalInput").ap()
    d_inpw = nc.dram_tensor("inpw", [D, PROJ], F32, kind="ExternalInput").ap()
    d_inpb = nc.dram_tensor("inpb", [128, 3], F32, kind="ExternalInput").ap()
    d_alog = nc.dram_tensor("alog", [DS, 1], F32, kind="ExternalInput").ap()
    d_outpw = nc.dram_tensor("outpw", [DS, D], F32, kind="ExternalInput").ap()
    d_outpb = nc.dram_tensor("outpb", [1, D], F32, kind="ExternalInput").ap()
    d_gatew = nc.dram_tensor("gatew", [D, NE], F32, kind="ExternalInput").ap()
    d_gbias = nc.dram_tensor("gbias_tok", [128, NE], F32, kind="ExternalInput").ap()
    d_routerw = nc.dram_tensor("routerw", [D, NE], F32, kind="ExternalInput").ap()
    d_moe1 = nc.dram_tensor("moe1", [10 * D, D], F32, kind="ExternalInput").ap()
    d_moe3 = nc.dram_tensor("moe3", [10 * D, D], F32, kind="ExternalInput").ap()
    d_moe2 = nc.dram_tensor("moe2", [10 * D, D], F32, kind="ExternalInput").ap()
    d_wq = nc.dram_tensor("wq", [D, D], F32, kind="ExternalInput").ap()
    d_wk = nc.dram_tensor("wk", [D, D], F32, kind="ExternalInput").ap()
    d_wv = nc.dram_tensor("wv", [D, D], F32, kind="ExternalInput").ap()
    d_wo = nc.dram_tensor("wo", [D, D], F32, kind="ExternalInput").ap()
    d_w1 = nc.dram_tensor("l1w1", [NE * D, F1], F32, kind="ExternalInput").ap()
    d_b1 = nc.dram_tensor("l1b1", [128, NE * (F1 // 128)], F32, kind="ExternalInput").ap()
    d_w2 = nc.dram_tensor("l1w2", [NE * F1, D], F32, kind="ExternalInput").ap()
    d_b2sel = nc.dram_tensor("b2sel", [NE * NE, D], F32, kind="ExternalInput").ap()

    d_dbg = {}

    def dbg_tensor(name, shape):
        d_dbg[name] = nc.dram_tensor("dbg_" + name, shape, F32, kind="ExternalOutput").ap()
        return d_dbg[name]

    n_out = [1]
    n_dbgdma = [0]

    with ExitStack() as top:
        et = top.enter_context
        x_sb = et(nc.sbuf_tensor("x_sb", [128, KT * T], F32))
        xn = et(nc.sbuf_tensor("xn", [128, KT * T], BF16))
        rowi = et(nc.sbuf_tensor("rowi", [128, 128], BF16))
        identf = et(nc.sbuf_tensor("identf", [128, 128], F32))
        coli = et(nc.sbuf_tensor("coli", [128, 128], BF16))
        ident = et(nc.sbuf_tensor("ident", [128, 128], BF16))
        ebc = et(nc.sbuf_tensor("ebc", [8, NE * 128], BF16))
        sel0f = et(nc.sbuf_tensor("sel0f", [2, 128], F32))
        onesD = et(nc.sbuf_tensor("onesD", [128, 128], BF16))
        onesrow = et(nc.sbuf_tensor("onesrow", [2, CW], BF16))
        convw = et(nc.sbuf_tensor("convw_s", [128, KT * DCONV], F32))
        convb = et(nc.sbuf_tensor("convb_s", [128, KT], F32))
        inpb = et(nc.sbuf_tensor("inpb_s", [128, 3], F32))
        alog = et(nc.sbuf_tensor("alog_s", [DS, 1], F32))
        outpw_b = et(nc.sbuf_tensor("outpw_b", [DS, D], BF16))
        outpb_row = et(nc.sbuf_tensor("outpb_row", [1, D], BF16))
        gatew_b = et(nc.sbuf_tensor("gatew_b", [128, KT * NE], BF16))
        routerw_b = et(nc.sbuf_tensor("routerw_b", [128, KT * NE], BF16))
        gbias_tok = et(nc.sbuf_tensor("gbias_tok_s", [128, NE], F32))
        b1_sb = et(nc.sbuf_tensor("b1_sb", [128, NE * (F1 // 128)], F32))
        b2cur = [et(nc.sbuf_tensor(f"b2cur{i}", [8, D], BF16)) for i in range(2)]
        gt = et(nc.sbuf_tensor("gt", [8, T], BF16))
        NBLK0 = T // 64
        negE = et(nc.sbuf_tensor("negE", [128, NBLK0 * 64], BF16))
        fconst = et(nc.sbuf_tensor("fconst", [128, 16], F32))
        fconst_vals = {}

        def fc(val):
            assert val in fconst_vals, f"const {val} not pre-registered"
            return fconst[:, fconst_vals[val]:fconst_vals[val] + 1]

        for _v in (EPS_LN, 0.125, 0.5, 1.0 / 6.0):
            fconst_vals[_v] = len(fconst_vals)
        gb_sb = et(nc.sbuf_tensor("gb_sb", [128, T], BF16))

        mm = Ctr(et(nc.semaphore("mm")))
        ve = Ctr(et(nc.semaphore("ve")))
        sc = Ctr(et(nc.semaphore("sc")))
        gp = Ctr(et(nc.semaphore("gp")))
        s_ldx = et(nc.semaphore("s_ldx"))
        s_ldg = et(nc.semaphore("s_ldg"))
        s_b2 = et(nc.semaphore("s_b2"))
        s_b2f = Ctr(et(nc.semaphore("s_b2f")))
        s_out = et(nc.semaphore("s_out"))
        s_dbg = et(nc.semaphore("s_dbg"))

        ring = Ring(nc, top, nslots=2, slot_elems=4096)
        p = Prog(nc)

        def dbg_fence():
            if n_dbgdma[0]:
                for st in (p.v, p.s, p.t):
                    st.wait_dma(s_dbg, 16 * n_dbgdma[0])

        def dump_dbg(name, sb, shape, ctrval, prows=128):
            dd = dbg_tensor(name, shape)
            n_out[0] += 1
            n_dbgdma[0] += 1
            ctr, val = ctrval
            p.g.wait(ctr, val)
            kk = shape[0] // prows

            def dump(g, dd=dd, sb=sb, kk=kk, prows=prows):
                if kk > 1:
                    g.dma_start(out=dd.rearrange("(k p) t -> p k t", p=prows),
                                in_=sb[:].rearrange("p (k t) -> p k t", k=kk)
                                ).then_inc(s_dbg, 16)
                else:
                    g.dma_start(out=dd, in_=sb[0:prows, :]).then_inc(s_dbg, 16)
            p.g.ops.append(dump)

        n_ldx = 7

        def _init_sync(s):
            s.dma_start(out=x_sb[:].rearrange("p (k t) -> p k t", k=KT),
                        in_=d_x.rearrange("(k p) t -> p k t", p=128)).then_inc(s_ldx, 16)
            for dst, src in [(convw, d_convw), (convb, d_convb), (inpb, d_inpb),
                             (alog, d_alog), (gbias_tok, d_gbias), (b1_sb, d_b1)]:
                s.dma_start(out=dst[:], in_=src).then_inc(s_ldx, 16)
        p.sy.ops.append(_init_sync)

        n_ldg = 4

        def _init_g(g):
            for dst, src, kk in [(gatew_b, d_gatew, KT), (routerw_b, d_routerw, KT)]:
                g.dma_start(out=dst[:].rearrange("p (k m) -> p k m", k=kk),
                            in_=src.rearrange("(k p) m -> p k m", p=128)).then_inc(s_ldg, 16)
            g.dma_start(out=outpw_b[:], in_=d_outpw).then_inc(s_ldg, 16)
            g.dma_start(out=outpb_row[:], in_=d_outpb).then_inc(s_ldg, 16)
            g.iota(rowi[:, :], [[0, 128]], channel_multiplier=1,
                   allow_small_or_imprecise_dtypes=True)
            g.iota(coli[:, :], [[1, 128]], channel_multiplier=0,
                   allow_small_or_imprecise_dtypes=True)
            g.drain()
        p.g.ops.append(_init_g)
        p.g.bar(gp)
        GP_IOTA = gp.n

        p.v.wait(gp, GP_IOTA)

        def _init_v(v):
            v.tensor_tensor(ident[:], rowi[:], coli[:], ALU.is_equal)
            v.tensor_tensor(identf[:], rowi[:], coli[:], ALU.is_equal)
            for e in range(NE):
                v.tensor_scalar(ebc[:, e * 128:(e + 1) * 128], rowi[0:8, :],
                                float(e), None, ALU.is_equal)
            v.memset(sel0f[:, :], 0.0)
            v.drain()
            v.memset(sel0f[0:1, :], 1.0)
            v.memset(onesD[:], 1.0 / D)
            v.memset(onesrow[:], 0.0)
            v.drain()
            v.memset(onesrow[0:1, :], 1.0)
            for val, i in fconst_vals.items():
                v.memset(fconst[:, i:i + 1], float(val))
            v.drain()
        p.v.ops.append(_init_v)
        p.v.bar(ve)
        VE_CONST = ve.n

        with nc.sbuf_tensor("blkx_tmp", [128, NBLK0 * 64], BF16) as blkx, \
                nc.sbuf_tensor("rmod_tmp", [128, 1], BF16) as rmod:
            p.g.ops.append(lambda g, blkx=blkx: g.iota(
                blkx[:, :], [[1, NBLK0], [0, 64]], channel_multiplier=0,
                allow_small_or_imprecise_dtypes=True))
            gp_bx = p.g.bar(gp)
            p.v.wait(gp, gp_bx)
            p.v.emit(lambda v, rmod=rmod: v.tensor_scalar(
                rmod[:], rowi[:, 0:1], 64.0, None, ALU.is_ge))
            p.v.drain()
            p.v.emit(lambda v, rmod=rmod: v.scalar_tensor_tensor(
                rmod[:], rmod[:], -64.0, rowi[:, 0:1], ALU.mult, ALU.add))
            p.v.drain()
            p.v.emit(lambda v, blkx=blkx, rmod=rmod: v.tensor_tensor(
                negE[:], bass.AP(rmod, 0, [[1, 128], [0, NBLK0 * 64]]),
                blkx[:], ALU.is_equal))
            p.v.drain()
            p.v.emit(lambda v: v.tensor_scalar(negE[:], negE[:], -1.0, None, ALU.mult))
            p.v.bar(ve)
        VE_CONST = ve.n

        def wait_x(st):
            st.wait_dma(s_ldx, 16 * n_ldx)

        def wait_consts(st):
            st.wait_dma(s_ldg, 16 * n_ldg)
            st.wait(ve, VE_CONST)

        def xv(k, c, w=None):
            w = CW if w is None else w
            return x_sb[:, k * T + c * CW: k * T + c * CW + w]

        def xnv(k, c, w=None):
            w = CW if w is None else w
            return xn[:, k * T + c * CW: k * T + c * CW + w]

        p.flush("init")

        def ln_phase(lname):
            dbg_fence()
            with ExitStack() as ctx:
                e = ctx.enter_context
                xstage = [e(nc.sbuf_tensor(f"{lname}_xs{i}", [128, KT * CW], BF16)) for i in range(2)]
                sqstage = [e(nc.sbuf_tensor(f"{lname}_qs{i}", [128, KT * CW], BF16)) for i in range(2)]
                m_sb = [e(nc.sbuf_tensor(f"{lname}_m{i}", [128, CW], F32)) for i in range(2)]
                vtmp = [e(nc.sbuf_tensor(f"{lname}_v{i}", [128, CW], F32)) for i in range(2)]
                xtmp = [e(nc.sbuf_tensor(f"{lname}_t{i}", [128, CW], F32)) for i in range(2)]
                pm = [Bank(e(nc.psum_tensor(f"{lname}_pm{i}", [128, CW], F32))) for i in range(2)]
                ps2 = [Bank(e(nc.psum_tensor(f"{lname}_ps{i}", [128, CW], F32))) for i in range(2)]
                stage_guard = [None, None]
                msb_guard = [None, None]

                for c in range(NCH):
                    b = c % 2
                    wait_x(p.v)
                    if stage_guard[b] is not None:
                        p.v.wait(mm, stage_guard[b])
                    for k in range(KT):
                        p.v.emit(lambda v, k=k, b=b, c=c: v.tensor_copy(
                            xstage[b][:, k * CW:(k + 1) * CW], xv(k, c)))
                    p.v.drain()
                    for k in range(KT):
                        p.v.emit(lambda v, k=k, b=b: v.tensor_mul(
                            sqstage[b][:, k * CW:(k + 1) * CW],
                            xstage[b][:, k * CW:(k + 1) * CW],
                            xstage[b][:, k * CW:(k + 1) * CW]))
                    p.v.drain()
                    ve_stage = p.v.bar(ve)

                    wait_consts(p.t)
                    p.t.wait(ve, ve_stage)
                    pm[b].acquire(p.t)
                    ps2[b].acquire(p.t)
                    for k in range(KT):
                        p.t.emit(lambda t, k=k, b=b: t.matmul(
                            pm[b].t[:], onesD[:], xstage[b][:, k * CW:(k + 1) * CW],
                            start=(k == 0), stop=(k == KT - 1)))
                    for k in range(KT):
                        p.t.emit(lambda t, k=k, b=b: t.matmul(
                            ps2[b].t[:], onesD[:], sqstage[b][:, k * CW:(k + 1) * CW],
                            start=(k == 0), stop=(k == KT - 1)))
                    mm_stat = p.t.bar(mm)
                    stage_guard[b] = mm_stat

                    p.s.wait(mm, mm_stat)
                    if msb_guard[b] is not None:
                        p.s.wait(ve, msb_guard[b])
                    p.s.emit(lambda s, b=b: s.activation(m_sb[b][:], pm[b].t[:], AF.Copy))
                    p.s.emit(lambda s, b=b: s.activation(vtmp[b][:], pm[b].t[:], AF.Square))
                    p.s.drain()
                    sc_m = p.s.bar(sc)

                    p.v.wait(sc, sc_m)
                    p.v.emit(lambda v, b=b: v.tensor_sub(vtmp[b][:], ps2[b].t[:], vtmp[b][:]))
                    p.v.drain()
                    ve_v = p.v.bar(ve)
                    ps2[b].guard(ve, ve_v)
                    pm[b].guard(sc, sc_m)

                    p.s.wait(ve, ve_v)
                    p.s.emit(lambda s, b=b: s.activation(vtmp[b][:], vtmp[b][:], AF.Sqrt, bias=fc(EPS_LN)))
                    p.s.drain()
                    sc_sd = p.s.bar(sc)

                    p.v.wait(sc, sc_sd)
                    p.v.emit(lambda v, b=b: v.reciprocal(vtmp[b][:], vtmp[b][:]))
                    p.v.drain()
                    for k in range(KT):
                        p.v.emit(lambda v, k=k, b=b, c=c: v.tensor_sub(
                            xtmp[b][:], xv(k, c), m_sb[b][:]))
                        p.v.drain()
                        p.v.emit(lambda v, k=k, b=b, c=c: v.tensor_mul(
                            xnv(k, c), xtmp[b][:], vtmp[b][:]))
                        p.v.drain()
                    ve_done = p.v.bar(ve)
                    msb_guard[b] = ve_done

                if "xn" + lname[-1] in debug:
                    dump_dbg("xn" + lname[-1], xn, [D, T], (ve, ve.n))
                p.flush(lname)

        ln_phase("ln0")
        _stop = [False]

        def stopped(name):
            if _stop[0] or stop_after == name:
                _stop[0] = True
            return _stop[0] and stop_after != name

        def routing_phase(lname, wsb, mode):
            dbg_fence()
            with ExitStack() as ctx:
                e = ctx.enter_context
                lgb = e(nc.sbuf_tensor(f"{lname}_lgb", [8, T], BF16))
                NC8 = NTT * NE
                lg_tok = e(nc.sbuf_tensor(f"{lname}_lgt", [128, NC8], F32))
                w_a = e(nc.sbuf_tensor(f"{lname}_wa", [128, NC8], F32))
                w_b = e(nc.sbuf_tensor(f"{lname}_wb", [128, NC8], F32))
                w_m = e(nc.sbuf_tensor(f"{lname}_wm", [128, NTT], F32))
                w_m2 = e(nc.sbuf_tensor(f"{lname}_wm2", [128, NTT], F32))
                gates_tok = e(nc.sbuf_tensor(f"{lname}_gk", [128, NC8], BF16))
                w_tmp = e(nc.sbuf_tensor(f"{lname}_wt", [128, NC8], F32))
                plg = Bank(e(nc.psum_tensor(f"{lname}_plg", [8, CW], F32)))
                ptr = [Bank(e(nc.psum_tensor(f"{lname}_ptr{i}", [128, NE], BF16))) for i in range(2)]
                ptr2 = [Bank(e(nc.psum_tensor(f"{lname}_pt2{i}", [8, 128], BF16))) for i in range(2)]

                def v3(tsb, last=NE):
                    return bass.AP(tsb, 0, [[NC8 if last == NE else NTT, 128],
                                            [last, NTT], [1, last]])

                def vb(tsb):
                    return bass.AP(tsb, 0, [[NTT, 128], [1, NTT], [0, NE]])

                for c in range(NCH):
                    wait_consts(p.t)
                    plg.acquire(p.t)
                    for k in range(KT):
                        p.t.emit(lambda t, k=k, c=c: t.matmul(
                            plg.t[:], wsb[:, k * NE:(k + 1) * NE], xnv(k, c),
                            start=(k == 0), stop=(k == KT - 1)))
                    mm_lg = p.t.bar(mm)
                    p.s.wait(mm, mm_lg)
                    p.s.emit(lambda s, c=c: s.activation(
                        lgb[:, c * CW:(c + 1) * CW], plg.t[:], AF.Copy))
                    sc_lg = p.s.bar(sc)
                    plg.guard(sc, sc_lg)
                for tt in range(NTT):
                    b = tt % 2
                    p.t.wait(sc, sc.n)
                    ptr[b].acquire(p.t)
                    p.t.emit(lambda t, tt=tt, b=b: t.transpose(
                        ptr[b].t[:], lgb[:, tt * 128:(tt + 1) * 128], ident[0:8, 0:8]))
                    mm_tr = p.t.bar(mm)
                    p.v.wait(mm, mm_tr)
                    p.v.emit(lambda v, tt=tt, b=b: v.tensor_copy(
                        lg_tok[:, tt * NE:(tt + 1) * NE], ptr[b].t[:]))
                    ve_tr = p.v.bar(ve)
                    ptr[b].guard(ve, ve_tr)
                V, S = p.v, p.s
                V.drain()
                V.emit(lambda v: v.tensor_reduce(v3(w_m, 1), v3(lg_tok), axis=AX.X, op=ALU.max))
                V.drain()
                if mode == "l0":
                    V.emit(lambda v: v.tensor_tensor(w_a[:], lg_tok[:], vb(w_m), ALU.subtract))
                    ve_d = V.bar(ve)
                    S.wait(ve, ve_d)
                    S.emit(lambda s: s.activation(w_a[:], w_a[:], AF.Exp))
                    sc_e = S.bar(sc)
                    V.wait(sc, sc_e)
                    V.emit(lambda v: v.tensor_reduce(v3(w_m, 1), v3(w_a), axis=AX.X, op=ALU.add))
                    V.drain()
                    V.emit(lambda v: v.reciprocal(w_m[:], w_m[:]))
                    V.drain()
                    V.emit(lambda v: v.tensor_tensor(w_a[:], w_a[:], vb(w_m), ALU.mult))
                    V.drain()
                    V.emit(lambda v: v.tensor_tensor(
                        w_b[:], w_a[:],
                        bass.AP(gbias_tok, 0, [[NE, 128], [0, NTT], [1, NE]]), ALU.add))
                    V.drain()
                    sel = w_b
                else:
                    sel = lg_tok
                V.emit(lambda v: v.tensor_reduce(v3(w_m, 1), v3(sel), axis=AX.X, op=ALU.max))
                V.drain()
                V.emit(lambda v: v.tensor_tensor(w_tmp[:], sel[:], vb(w_m), ALU.is_equal))
                V.drain()
                V.emit(lambda v: v.scalar_tensor_tensor(w_tmp[:], w_tmp[:], -1e30, sel[:],
                                                        ALU.mult, ALU.add))
                V.drain()
                V.emit(lambda v: v.tensor_reduce(v3(w_m2, 1), v3(w_tmp), axis=AX.X, op=ALU.max))
                V.drain()
                V.emit(lambda v: v.tensor_tensor(w_tmp[:], sel[:], vb(w_m2), ALU.is_ge))
                V.drain()
                if mode == "l0":
                    V.emit(lambda v: v.tensor_tensor(gates_tok[:], w_a[:], w_tmp[:], ALU.mult))
                    V.drain()
                else:
                    V.emit(lambda v: v.tensor_tensor(w_a[:], lg_tok[:], vb(w_m), ALU.subtract))
                    ve_d = V.bar(ve)
                    S.wait(ve, ve_d)
                    S.emit(lambda s: s.activation(w_a[:], w_a[:], AF.Exp))
                    sc_e = S.bar(sc)
                    V.wait(sc, sc_e)
                    V.emit(lambda v: v.tensor_tensor(w_a[:], w_a[:], w_tmp[:], ALU.mult))
                    V.drain()
                    V.emit(lambda v: v.tensor_reduce(v3(w_m, 1), v3(w_a), axis=AX.X, op=ALU.add))
                    V.drain()
                    V.emit(lambda v: v.reciprocal(w_m[:], w_m[:]))
                    V.drain()
                    V.emit(lambda v: v.tensor_tensor(gates_tok[:], w_a[:], vb(w_m), ALU.mult))
                    V.drain()
                ve_gk = V.bar(ve)
                for tt in range(NTT):
                    b = tt % 2
                    p.t.wait(ve, ve_gk)
                    ptr2[b].acquire(p.t)
                    p.t.emit(lambda t, tt=tt, b=b: t.transpose(
                        ptr2[b].t[:], gates_tok[:, tt * NE:(tt + 1) * NE], ident[:]))
                    mm_tb = p.t.bar(mm)
                    p.v.wait(mm, mm_tb)
                    p.v.emit(lambda v, tt=tt, b=b: v.tensor_copy(
                        gt[:, tt * 128:(tt + 1) * 128], ptr2[b].t[:]))
                    ve_tb = p.v.bar(ve)
                    ptr2[b].guard(ve, ve_tb)
                if "gt" + lname[-1] in debug:
                    dump_dbg("gt" + lname[-1], gt, [8, T], (ve, ve.n), prows=8)
                p.flush(lname)

        Th = T // 2
        if stop_after == "ln0":
            pass
        with ExitStack() as l0ctx:
            el0 = l0ctx.enter_context
            pj = [el0(nc.sbuf_tensor(f"pj{i}", [128, T], F32)) for i in range(3)]

            with ExitStack() as ctx:
                e = ctx.enter_context
                xcbh = e(nc.sbuf_tensor("xcbh", [128, KT * Th], BF16))
                acc = e(nc.sbuf_tensor("acc", [128, Th], F32))
                pjb = [Bank(e(nc.psum_tensor(f"pjb{i}", [128, CW], F32))) for i in range(6)]
                h_inpw = ring.load(p, d_inpw, 128, KT, PROJ)
                MWS = [128, 128, 64]
                sc_conv = [0] * KT
                last_acc = [None]
                xcb_guard = [None]
                for half in range(2):
                    c0 = half * Th
                    c1 = c0 + Th
                    if xcb_guard[0] is not None:
                        p.v.wait(mm, xcb_guard[0])
                        p.s.wait(mm, xcb_guard[0])
                    for k in range(KT):
                        cwv = convw[:, k * DCONV: (k + 1) * DCONV]
                        xk = lambda a, b, k=k: xn[:, k * T + a: k * T + b]
                        p.v.drain()
                        p.v.emit(lambda v, k=k, c0=c0, c1=c1, cwv=cwv: v.tensor_scalar(
                            acc[:], xk(c0, c1, k), cwv[:, 1:2], None, ALU.mult))
                        p.v.drain()
                        if c0 == 0:
                            p.v.emit(lambda v, k=k, c1=c1, cwv=cwv: v.scalar_tensor_tensor(
                                acc[:, 1:Th], xk(0, c1 - 1, k), cwv[:, 0:1],
                                acc[:, 1:Th], ALU.mult, ALU.add))
                        else:
                            p.v.emit(lambda v, k=k, c0=c0, c1=c1, cwv=cwv: v.scalar_tensor_tensor(
                                acc[:], xk(c0 - 1, c1 - 1, k), cwv[:, 0:1],
                                acc[:], ALU.mult, ALU.add))
                        p.v.drain()
                        if c1 == T:
                            p.v.emit(lambda v, k=k, c0=c0, cwv=cwv: v.scalar_tensor_tensor(
                                acc[:, 0:Th - 1], xk(c0 + 1, T, k), cwv[:, 2:3],
                                acc[:, 0:Th - 1], ALU.mult, ALU.add))
                            p.v.drain()
                            p.v.emit(lambda v, k=k, c0=c0, cwv=cwv: v.scalar_tensor_tensor(
                                acc[:, 0:Th - 2], xk(c0 + 2, T, k), cwv[:, 3:4],
                                acc[:, 0:Th - 2], ALU.mult, ALU.add))
                        else:
                            p.v.emit(lambda v, k=k, c0=c0, c1=c1, cwv=cwv: v.scalar_tensor_tensor(
                                acc[:], xk(c0 + 1, c1 + 1, k), cwv[:, 2:3],
                                acc[:], ALU.mult, ALU.add))
                            p.v.drain()
                            p.v.emit(lambda v, k=k, c0=c0, c1=c1, cwv=cwv: v.scalar_tensor_tensor(
                                acc[:], xk(c0 + 2, c1 + 2, k), cwv[:, 3:4],
                                acc[:], ALU.mult, ALU.add))
                        p.v.drain()
                        p.v.emit(lambda v, k=k: v.tensor_scalar(
                            xcbh[:, k * Th:(k + 1) * Th], acc[:],
                            convb[:, k:k + 1], None, ALU.add))
                        ve_conv = p.v.bar(ve)
                        sc_conv[k] = ve_conv
                        last_acc[0] = ve_conv
                        if c1 == T:
                            p.v.drain()
                            p.v.emit(lambda v, k=k: v.memset(
                                xcbh[:, (k + 1) * Th - 1:(k + 1) * Th], 0.0))
                            p.v.drain()
                    ve_xcb = p.v.bar(ve)
                    CWH = min(CW, Th)
                    for ch in range(Th // CWH):
                        for m in range(3):
                            bank = pjb[(ch % 2) * 3 + m]
                            ring.use(p.t, h_inpw)
                            p.t.wait(ve, ve_xcb)
                            bank.acquire(p.t)
                            mw = MWS[m]
                            for k in range(KT):
                                p.t.emit(lambda t, k=k, m=m, ch=ch, bank=bank, mw=mw, CWH=CWH: t.matmul(
                                    bank.t[0:mw, 0:CWH], ring.lhsT(h_inpw, k, m * 128, mw),
                                    xcbh[:, k * Th + ch * CWH: k * Th + (ch + 1) * CWH],
                                    start=(k == 0), stop=(k == KT - 1)))
                            mm_pj = p.t.bar(mm)
                            p.v.wait(mm, mm_pj)
                            gcol = half * Th + ch * CWH
                            p.v.emit(lambda v, m=m, bank=bank, mw=mw, gcol=gcol, CWH=CWH: v.tensor_scalar(
                                pj[m][0:mw, gcol:gcol + CWH], bank.t[0:mw, 0:CWH],
                                inpb[0:mw, m:m + 1], None, ALU.add))
                            ve_pj = p.v.bar(ve)
                            bank.guard(ve, ve_pj)
                    xcb_guard[0] = mm.n
                ring.release(p.t, h_inpw)
                if "proj" in debug:
                    p0d = dbg_tensor("proj", [3 * 128, T])
                    n_out[0] += 3
                    n_dbgdma[0] += 3
                    p.g.wait(ve, ve.n)

                    def dump_proj(g):
                        for i, rows in ((0, 128), (1, 128), (2, 64)):
                            g.dma_start(out=p0d[i * 128:i * 128 + rows, :],
                                        in_=pj[i][0:rows, :]).then_inc(s_dbg, 16)
                    p.g.ops.append(dump_proj)
                p.flush("l0a")

            routing_phase("rt0", gatew_b, "l0")

            with ExitStack() as ctx:
                e = ctx.enter_context
                MH = (D // 2) // 128
                hh = e(nc.sbuf_tensor("hh0", [128, MH * T], BF16))
                ma = e(nc.sbuf_tensor("ma", [DS, T], F32))
                mb = e(nc.sbuf_tensor("mb", [DS, T], F32))
                yb = e(nc.sbuf_tensor("yb", [DS, T], BF16))
                asml = e(nc.sbuf_tensor("asml", [DS, 8], F32))
                silu_t = [e(nc.sbuf_tensor(f"silu{i}", [128, CW], BF16)) for i in range(2)]
                H1 = [Bank(e(nc.psum_tensor(f"h1_{i}", [128, CW], F32))) for i in range(2)]
                H3 = [Bank(e(nc.psum_tensor(f"h3_{i}", [128, CW], F32))) for i in range(2)]
                YB = [Bank(e(nc.psum_tensor(f"yb_{i}", [128, CW], F32))) for i in range(2)]
                GB = Bank(e(nc.psum_tensor("gbp", [128, CW], F32)))
                OP = Bank(e(nc.psum_tensor("opp", [128, CW], F32)))

                slotA = pj[0][0:DS, :]
                Be = pj[0][DS:2 * DS, :]
                slotB = pj[1][0:DS, :]
                Ce = pj[1][DS:2 * DS, :]
                slotC = pj[2][0:DS, :]
                delta = slotA


                def run_interleaved_l0(a, b):
                    gens = [a, b]
                    while gens:
                        nxt = []
                        for g in gens:
                            try:
                                next(g)
                                nxt.append(g)
                            except StopIteration:
                                pass
                        gens = nxt

                ve_yb_box = [None]

                def gen_mamba():
                    S, V = p.s, p.v
                    if n_dbgdma[0]:
                        V.wait_dma(s_dbg, 16 * n_dbgdma[0])
                        S.wait_dma(s_dbg, 16 * n_dbgdma[0])
                    S.emit(lambda s: s.activation(ma[:], delta, AF.Abs))
                    S.drain()
                    S.emit(lambda s: s.activation(ma[:], ma[:], AF.Exp, scale=-1.0))
                    S.drain()
                    S.emit(lambda s: s.activation(ma[:], ma[:], AF.Ln, bias=1.0))
                    S.drain()
                    S.emit(lambda s: s.activation(asml[:, 0:1], alog[:], AF.Exp))
                    sc_spl = S.bar(sc)
                    yield

                    V.emit(lambda v: v.tensor_scalar_max(mb[:], delta, 0.0))
                    V.drain()
                    V.wait(sc, sc_spl)
                    V.emit(lambda v: v.tensor_add(mb[:], mb[:], ma[:]))
                    V.drain()
                    V.emit(lambda v: v.tensor_scalar(asml[:, 1:2], asml[:, 0:1], -1.0, None, ALU.mult))
                    V.drain()
                    V.emit(lambda v: v.tensor_scalar(asml[:, 2:3], asml[:, 0:1], EPS_A, None, ALU.is_lt))
                    V.drain()
                    V.emit(lambda v: v.tensor_scalar(asml[:, 3:4], asml[:, 2:3], -1.0, 1.0, ALU.mult, ALU.add))
                    V.drain()
                    V.emit(lambda v: v.tensor_mul(asml[:, 4:5], asml[:, 1:2], asml[:, 3:4]))
                    V.drain()
                    V.emit(lambda v: v.tensor_add(asml[:, 4:5], asml[:, 4:5], asml[:, 2:3]))
                    V.drain()
                    V.emit(lambda v: v.reciprocal(asml[:, 5:6], asml[:, 4:5]))
                    V.drain()
                    V.emit(lambda v: v.tensor_scalar(ma[:], mb[:], asml[:, 1:2], None, ALU.mult))
                    V.drain()
                    V.emit(lambda v: v.tensor_scalar(slotA, ma[:], 1.0 / 6.0, 0.5, ALU.mult, ALU.add))
                    V.drain()
                    V.emit(lambda v: v.tensor_mul(slotA, slotA, ma[:]))
                    V.drain()
                    V.emit(lambda v: v.tensor_scalar_add(slotA, slotA, 1.0))
                    ve_tay = V.bar(ve)
                    yield
                    S.wait(ve, ve_tay)
                    S.emit(lambda s: s.activation(mb[:], ma[:], AF.Exp))
                    sc_abar = S.bar(sc)
                    yield
                    V.wait(sc, sc_abar)
                    V.emit(lambda v: v.tensor_scalar(ma[:], mb[:], -1.0, asml[:, 5:6], ALU.add, ALU.mult))
                    V.drain()
                    V.emit(lambda v: v.tensor_scalar(ma[:], ma[:], asml[:, 3:4], None, ALU.mult))
                    V.drain()
                    V.emit(lambda v: v.tensor_scalar(slotA, slotA, asml[:, 2:3], None, ALU.mult))
                    V.drain()
                    V.emit(lambda v: v.tensor_add(ma[:], ma[:], slotA))
                    V.drain()
                    V.emit(lambda v: v.tensor_copy(slotA, Be))
                    V.drain()
                    V.emit(lambda v: v.tensor_add(slotB, slotA, slotB))
                    V.drain()
                    V.emit(lambda v: v.scalar_tensor_tensor(slotB, slotB, 0.5, ma[:], ALU.mult, ALU.mult))
                    V.drain()
                    V.emit(lambda v: v.tensor_tensor_scan(slotA, mb[:], slotB, 0.0, ALU.mult, ALU.add))
                    V.drain()
                    V.emit(lambda v: v.tensor_copy(mb[:], Ce))
                    V.drain()
                    V.emit(lambda v: v.tensor_add(mb[:], mb[:], slotC))
                    V.drain()
                    V.emit(lambda v: v.scalar_tensor_tensor(mb[:], mb[:], 0.5, slotA, ALU.mult, ALU.mult))
                    V.drain()
                    V.emit(lambda v: v.tensor_copy(yb[:], mb[:]))
                    ve_yb = V.bar(ve)
                    yield
                    if "ystate" in debug:
                        dump_dbg("ystate", yb, [DS, T], (ve, ve_yb), prows=DS)

                    ve_yb_box[0] = ve_yb
                    wait_consts(p.t)
                    p.t.wait(ve, ve_yb)
                    for m in range(KT):
                        for c in range(NCH):
                            OP.acquire(p.t)
                            p.t.emit(lambda t, m=m, c=c: t.matmul(
                                OP.t[:], outpw_b[:, m * 128:(m + 1) * 128],
                                yb[:, c * CW:(c + 1) * CW], start=True, stop=False))
                            p.t.emit(lambda t, m=m, c=c: t.matmul(
                                OP.t[:], outpb_row[0:1, m * 128:(m + 1) * 128],
                                onesrow[0:1, :], start=False, stop=True))
                            mm_op = p.t.bar(mm)
                            p.v.wait(mm, mm_op)
                            p.v.emit(lambda v, m=m, c=c: v.tensor_add(
                                xv(m, c), xv(m, c), OP.t[:]))
                            ve_op = p.v.bar(ve)
                            OP.guard(ve, ve_op)
                            yield

                def gen_moe():
                    NPASS = 10
                    hh_guard = [None]
                    silu_guard = [None, None]
                    for psi in range(NPASS):
                        gated = psi < 8
                        if gated:
                            for c in range(NCH):
                                GB.acquire(p.t)
                                p.t.wait(ve, VE_CONST)
                                p.t.emit(lambda t, psi=psi, c=c: t.matmul(
                                    GB.t[:], ebc[:, psi * 128:(psi + 1) * 128],
                                    gt[0:8, c * CW:(c + 1) * CW], start=True, stop=True))
                                mm_gb = p.t.bar(mm)
                                p.v.wait(mm, mm_gb)
                                p.v.drain()
                                p.v.emit(lambda v, c=c: v.tensor_copy(
                                    gb_sb[:, c * CW:(c + 1) * CW], GB.t[:]))
                                ve_gb = p.v.bar(ve)
                                GB.guard(ve, ve_gb)
                        for fh in range(2):
                            h_w1 = ring.load(p, d_moe1[psi * D:(psi + 1) * D,
                                                       fh * (D // 2):(fh + 1) * (D // 2)],
                                             128, KT, D // 2)
                            h_w3 = ring.load(p, d_moe3[psi * D:(psi + 1) * D,
                                                       fh * (D // 2):(fh + 1) * (D // 2)],
                                             128, KT, D // 2)
                            for m in range(MH):
                                for c in range(NCH):
                                    i = (m * NCH + c) % 2
                                    ring.use(p.t, h_w1)
                                    ring.use(p.t, h_w3)
                                    H1[i].acquire(p.t)
                                    H3[i].acquire(p.t)
                                    for k in range(KT):
                                        p.t.emit(lambda t, k=k, m=m, c=c, i=i, h=h_w1: t.matmul(
                                            H1[i].t[:], ring.lhsT(h, k, m * 128, 128), xnv(k, c),
                                            start=(k == 0), stop=(k == KT - 1)))
                                    for k in range(KT):
                                        p.t.emit(lambda t, k=k, m=m, c=c, i=i, h=h_w3: t.matmul(
                                            H3[i].t[:], ring.lhsT(h, k, m * 128, 128), xnv(k, c),
                                            start=(k == 0), stop=(k == KT - 1)))
                                    mm_h = p.t.bar(mm)
                                    p.s.wait(mm, mm_h)
                                    if silu_guard[i] is not None:
                                        p.s.wait(ve, silu_guard[i])
                                    p.s.emit(lambda s, i=i: s.activation(
                                        silu_t[i][:], H1[i].t[:], AF.Silu))
                                    sc_si = p.s.bar(sc)
                                    H1[i].guard(sc, sc_si)
                                    p.v.wait(sc, sc_si)
                                    if hh_guard[0] is not None:
                                        p.v.wait(mm, hh_guard[0])
                                    hcol = m * T + c * CW
                                    p.v.emit(lambda v, i=i, hcol=hcol: v.tensor_mul(
                                        hh[:, hcol:hcol + CW], silu_t[i][:], H3[i].t[:]))
                                    ve_hh = p.v.bar(ve)
                                    H3[i].guard(ve, ve_hh)
                                    silu_guard[i] = ve_hh
                                    yield
                                    if gated:
                                        p.v.drain()
                                        p.v.emit(lambda v, c=c, hcol=hcol: v.tensor_mul(
                                            hh[:, hcol:hcol + CW], hh[:, hcol:hcol + CW],
                                            gb_sb[:, c * CW:(c + 1) * CW]))
                                        ve_hh = p.v.bar(ve)
                            ve_hh_all = ve.n
                            ring.release(p.t, h_w1)
                            ring.release(p.t, h_w3)
                            NSL = max(1, D // CW)
                            CW2 = min(CW, D)
                            r0 = psi * D + fh * (D // 2)
                            for sl in range(NSL):
                                h_w2 = ring.load(p, d_moe2[r0:r0 + D // 2,
                                                           sl * CW2:(sl + 1) * CW2],
                                                 128, MH, CW2)
                                for ml in range(CW2 // 128):
                                    for c in range(NCH):
                                        i = (ml * NCH + c) % 2
                                        mglob = sl * (CW2 // 128) + ml
                                        ring.use(p.t, h_w2)
                                        p.t.wait(ve, ve_hh_all)
                                        YB[i].acquire(p.t)
                                        for k in range(MH):
                                            p.t.emit(lambda t, k=k, ml=ml, c=c, i=i, h=h_w2: t.matmul(
                                                YB[i].t[:], ring.lhsT(h, k, ml * 128, 128),
                                                hh[:, k * T + c * CW: k * T + (c + 1) * CW],
                                                start=(k == 0), stop=(k == MH - 1)))
                                        mm_y = p.t.bar(mm)
                                        p.v.wait(mm, mm_y)
                                        p.v.emit(lambda v, i=i, mglob=mglob, c=c: v.tensor_add(
                                            xv(mglob, c), xv(mglob, c), YB[i].t[:]))
                                        ve_y = p.v.bar(ve)
                                        YB[i].guard(ve, ve_y)
                                        yield
                                ring.release(p.t, h_w2)
                            hh_guard[0] = mm.n

                run_interleaved_l0(gen_mamba(), gen_moe())
                if "x1" in debug:
                    dump_dbg("x1", x_sb, [D, T], (ve, ve.n))
                p.flush("l0b")

        ln_phase("ln1")
        routing_phase("rt1", routerw_b, "l1")

        D2 = D // 2
        HHALF = NH // 2
        PPH = HHALF // 2
        BK = 64
        NBLK = T // 64
        BPC = CW // BK
        NQH = 4
        VG = NQH * 64
        NQ = NTT * NBLK
        with ExitStack() as ctx:
            e = ctx.enter_context
            dbg_fence()
            v_t = [e(nc.sbuf_tensor(f"v_t{i}", [128, NTT * VG], BF16)) for i in range(2)]
            o_pair = e(nc.sbuf_tensor("o_pair", [128, T], BF16))
            q2 = [e(nc.sbuf_tensor(f"q2_{i}", [128, T], BF16)) for i in range(2)]
            k2 = [e(nc.sbuf_tensor(f"k2_{i}", [128, T], BF16)) for i in range(2)]
            pt_e = [e(nc.sbuf_tensor(f"pt_e{i}", [128, CW], BF16)) for i in range(3)]
            pt_o = [e(nc.sbuf_tensor(f"pt_o{i}", [128, CW], BF16)) for i in range(3)]
            es = [e(nc.sbuf_tensor(f"es{i}", [128, CW], F32)) for i in range(2)]
            mu_sb = [e(nc.sbuf_tensor("mu0", [128, NQ], F32))] * 2
            sig_sb = [e(nc.sbuf_tensor("sg0", [128, NQ], F32))] * 2
            bw = [e(nc.sbuf_tensor(f"bw{i}", [128, NQ], F32)) for i in range(4)]
            bw.append(sig_sb[0])
            zt = e(nc.sbuf_tensor("zt", [128, NQ], F32))
            btr2 = [e(nc.sbuf_tensor(f"btr2_{i}", [128, T], BF16)) for i in range(2)]
            B = [Bank(e(nc.psum_tensor(f"atb{i}", [128, CW], F32))) for i in range(8)]
            OE, OO = B[4], B[5]

            p.v.emit(lambda v: v.memset(zt[:], 0.0))
            p.v.bar(ve)

            guards = dict(vt=[None, None], opair=None, pt=[None] * 3, es=[None, None],
                          q2=[None, None], btr=[None, None])

            def gen_prep(u):
                pg = u
                half = u // PPH
                pb = u % 2
                h_wq = ring.load(p, d_wq[:, pg * 128:(pg + 1) * 128], 128, KT, 128)
                h_wk = ring.load(p, d_wk[:, pg * 128:(pg + 1) * 128], 128, KT, 128)
                for c in range(NCH):
                    QB = B[6 + (c % 2)]
                    ring.use(p.t, h_wq)
                    QB.acquire(p.t)
                    if guards["q2"][pb] is not None:
                        p.t.wait(mm, guards["q2"][pb])
                    for k in range(KT):
                        p.t.emit(lambda t, k=k, c=c, QB=QB, h=h_wq: t.matmul(
                            QB.t[:], ring.lhsT(h, k, 0, 128), xnv(k, c),
                            start=(k == 0), stop=(k == KT - 1)))
                    mm_q = p.t.bar(mm)
                    p.s.wait(mm, mm_q)
                    p.s.emit(lambda s, c=c, QB=QB, pb=pb: s.activation(
                        q2[pb][:, c * CW:(c + 1) * CW], QB.t[:], AF.Copy, scale=fc(0.125)))
                    sc_q = p.s.bar(sc)
                    QB.guard(sc, sc_q)
                    yield
                    QB2 = B[6 + ((c + 1) % 2)]
                    ring.use(p.t, h_wk)
                    QB2.acquire(p.t)
                    for k in range(KT):
                        p.t.emit(lambda t, k=k, c=c, QB2=QB2, h=h_wk: t.matmul(
                            QB2.t[:], ring.lhsT(h, k, 0, 128), xnv(k, c),
                            start=(k == 0), stop=(k == KT - 1)))
                    mm_k = p.t.bar(mm)
                    p.s.wait(mm, mm_k)
                    p.s.emit(lambda s, c=c, QB2=QB2, pb=pb: s.activation(
                        k2[pb][:, c * CW:(c + 1) * CW], QB2.t[:], AF.Copy))
                    sc_k = p.s.bar(sc)
                    QB2.guard(sc, sc_k)
                    yield
                ring.release(p.t, h_wq)
                ring.release(p.t, h_wk)
                sc_qk = sc.n
                sti = 0
                for par in range(2):
                    rb0 = 64 * par
                    tp = (0, 0) if par == 0 else (64, 0)
                    pend_dve = [None]
                    for qt in range(NTT):
                        for kvc in range(NCH):
                            SB_ = B[6 + (sti % 2)]
                            sti += 1
                            p.t.wait(sc, sc_qk)
                            SB_.acquire(p.t)
                            p.t.emit(lambda t, rb0=rb0, qt=qt, kvc=kvc, SB_=SB_, tp=tp, pb=pb: t.matmul(
                                SB_.t[:], q2[pb][rb0:rb0 + 64, qt * 128:(qt + 1) * 128],
                                k2[pb][rb0:rb0 + 64, kvc * CW:(kvc + 1) * CW],
                                start=True, stop=True, tile_position=tp))
                            mm_st = p.t.bar(mm)
                            col = qt * NBLK + kvc * BPC
                            esb = sti % 2
                            p.s.wait(mm, mm_st)
                            if guards["es"][esb] is not None:
                                p.s.wait(ve, guards["es"][esb])
                            p.s.emit(lambda s, esb=esb, SB_=SB_: s.activation(
                                es[esb][:], SB_.t[:], AF.Exp))
                            sc_es = p.s.bar(sc)
                            SB_.guard(sc, sc_es)
                            if pend_dve[0] is not None:
                                pend_dve[0]()

                            def dve_stats(par=par, col=col, esb=esb, sc_es=sc_es):
                                p.v.wait(sc, sc_es)
                                p.v.emit(lambda v, par=par, col=col, esb=esb: v.tensor_reduce(
                                    bass.AP(mu_sb[par], col, [[NQ, 128], [1, BPC], [1, 1]]),
                                    es[esb][:].rearrange("p (b k) -> p b k", k=BK),
                                    axis=AX.X, op=ALU.max))
                                p.v.emit(lambda v, par=par, col=col, esb=esb: v.tensor_reduce(
                                    bass.AP(sig_sb[par], col, [[NQ, 128], [1, BPC], [1, 1]]),
                                    es[esb][:].rearrange("p (b k) -> p b k", k=BK),
                                    axis=AX.X, op=ALU.add))
                                ve_sg = p.v.bar(ve)
                                guards["es"][esb] = ve_sg
                            pend_dve[0] = dve_stats
                            yield
                    pend_dve[0]()
                    pend_dve[0] = None
                    V_, S_ = p.v, p.s
                    rb = 64 * par

                    def g3(tsb):
                        return tsb[:, 0:NQ].rearrange("p (q b) -> p q b", b=NBLK)
                    V_.drain()
                    for qt in range(NTT):
                        V_.emit(lambda v, par=par, qt=qt: v.tensor_tensor_scan(
                            bw[2][:, qt * NBLK:(qt + 1) * NBLK],
                            mu_sb[par][:, qt * NBLK:(qt + 1) * NBLK],
                            mu_sb[par][:, qt * NBLK:(qt + 1) * NBLK],
                            0.0, ALU.max, ALU.max))
                    ve_mx = V_.bar(ve)
                    S_.wait(ve, ve_mx)
                    S_.emit(lambda s: s.activation(bw[0][:, 0:NQ], bw[2][:, 0:NQ], AF.Ln))
                    sc_mx = S_.bar(sc)
                    S_.drain()
                    V_.wait(sc, sc_mx)
                    V_.emit(lambda v: v.memset(
                        bass.AP(bw[1], 0, [[NQ, 128], [NBLK, NTT], [1, 1]]), -1e30))
                    V_.drain()
                    V_.emit(lambda v: v.tensor_copy(
                        bass.AP(bw[1], 1, [[NQ, 128], [NBLK, NTT], [1, NBLK - 1]]),
                        bass.AP(bw[0], 0, [[NQ, 128], [NBLK, NTT], [1, NBLK - 1]])))
                    V_.drain()
                    V_.emit(lambda v: v.tensor_sub(bw[1][:, 0:NQ], bw[1][:, 0:NQ], bw[0][:, 0:NQ]))
                    ve_d = V_.bar(ve)
                    yield
                    S_.wait(ve, ve_d)
                    S_.emit(lambda s: s.activation(bw[2][:, 0:NQ], bw[1][:, 0:NQ], AF.Exp))
                    S_.drain()
                    S_.emit(lambda s: s.activation(bw[3][:, 0:NQ], bw[0][:, 0:NQ], AF.Exp, scale=-1.0))
                    sc_a = S_.bar(sc)
                    V_.wait(sc, sc_a)
                    V_.emit(lambda v, par=par: v.tensor_mul(
                        bw[3][:, 0:NQ], sig_sb[par][:, 0:NQ], bw[3][:, 0:NQ]))
                    V_.drain()
                    for qt in range(NTT):
                        V_.emit(lambda v, qt=qt: v.tensor_tensor_scan(
                            bw[4][:, qt * NBLK:(qt + 1) * NBLK],
                            bw[2][:, qt * NBLK:(qt + 1) * NBLK],
                            bw[3][:, qt * NBLK:(qt + 1) * NBLK],
                            0.0, ALU.mult, ALU.add))
                    ve_l = V_.bar(ve)
                    yield
                    S_.wait(ve, ve_l)
                    S_.emit(lambda s: s.activation(bw[2][:, 0:NQ], bw[4][:, 0:NQ], AF.Ln))
                    sc_lnl = S_.bar(sc)
                    V_.wait(sc, sc_lnl)
                    V_.emit(lambda v: v.tensor_sub(bw[1][:, 0:NQ], bw[1][:, 0:NQ], bw[2][:, 0:NQ]))
                    V_.drain()
                    V_.emit(lambda v: v.memset(
                        bass.AP(bw[1], 0, [[NQ, 128], [NBLK, NTT], [1, 1]]), 0.0))
                    V_.drain()
                    for qt in range(NTT):
                        V_.emit(lambda v, qt=qt: v.tensor_tensor_scan(
                            bw[3][:, qt * NBLK:(qt + 1) * NBLK],
                            bw[1][:, qt * NBLK:(qt + 1) * NBLK],
                            zt[:, qt * NBLK:(qt + 1) * NBLK],
                            0.0, ALU.add, ALU.add))
                    V_.drain()
                    V_.emit(lambda v: v.tensor_add(bw[0][:, 0:NQ], bw[0][:, 0:NQ], bw[2][:, 0:NQ]))
                    V_.drain()
                    V_.emit(lambda v: v.tensor_add(bw[0][:, 0:NQ], bw[0][:, 0:NQ], bw[3][:, 0:NQ]))
                    V_.drain()
                    V_.emit(lambda v: v.tensor_tensor(
                        g3(bw[0]), g3(bw[0]),
                        bass.AP(bw[3], NBLK - 1, [[NQ, 128], [NBLK, NTT], [0, NBLK]]),
                        ALU.subtract))
                    ve_b = V_.bar(ve)
                    yield
                    for qt in range(NTT):
                        TRP = B[6 + (qt % 2)]
                        p.t.wait(ve, ve_b)
                        TRP.acquire(p.t)
                        p.t.emit(lambda t, qt=qt, TRP=TRP: t.transpose(
                            TRP.t[0:NBLK, 0:128], bw[0][:, qt * NBLK:(qt + 1) * NBLK],
                            identf[:]))
                        mm_tr = p.t.bar(mm)
                        p.v.wait(mm, mm_tr)
                        if guards["btr"][pb] is not None:
                            p.v.wait(mm, guards["btr"][pb])
                        p.v.drain()
                        p.v.emit(lambda v, pb=pb, rb=rb, qt=qt, TRP=TRP: v.tensor_copy(
                            btr2[pb][rb:rb + NBLK, qt * 128:(qt + 1) * 128],
                            TRP.t[0:NBLK, 0:128]))
                        ve_bt = p.v.bar(ve)
                        TRP.guard(ve, ve_bt)
                        if qt % 4 == 3:
                            yield
                    p.v.wait(mm, mm.n)

            def gen_vprod(vg):
                vb_ = vg % 2
                h_wv = ring.load(p, d_wv[:, vg * VG:(vg + 1) * VG], 128, KT, VG)
                if guards["vt"][vb_] is not None:
                    p.v.wait(mm, guards["vt"][vb_])
                for tt in range(NTT):
                    VB = B[tt % 2]
                    ring.use(p.t, h_wv)
                    VB.acquire(p.t)
                    for k in range(KT):
                        p.t.emit(lambda t, k=k, tt=tt, VB=VB, h=h_wv: t.matmul(
                            VB.t[:, 0:VG], xn[:, k * T + tt * 128: k * T + (tt + 1) * 128],
                            ring.lhsT(h, k, 0, VG), start=(k == 0), stop=(k == KT - 1)))
                    mm_v = p.t.bar(mm)
                    p.v.wait(mm, mm_v)
                    p.v.emit(lambda v, tt=tt, VB=VB, vb_=vb_: v.tensor_copy(
                        v_t[vb_][:, tt * VG:(tt + 1) * VG], VB.t[:, 0:VG]))
                    ve_vt = p.v.bar(ve)
                    VB.guard(ve, ve_vt)
                    if tt % 2 == 1:
                        yield
                ring.release(p.t, h_wv)

            def gen_st(u):
                pg = u
                pb = u % 2
                pp_in_vg = u % 2
                ve_beta = ve.n

                vb_ = (u // 2) % 2

                def emit_o(pend):
                    kt, ip, sc_p = pend
                    p.t.wait(sc, sc_p)
                    p.t.emit(lambda t, kt=kt, ip=ip: t.matmul(
                        OE.t[0:64, :], v_t[vb_][:, kt * VG + (2 * pp_in_vg) * 64: kt * VG + (2 * pp_in_vg) * 64 + 64],
                        pt_e[ip][:], start=(kt == 0), stop=(kt == NTT - 1)))
                    p.t.emit(lambda t, kt=kt, ip=ip: t.matmul(
                        OO.t[0:64, :], v_t[vb_][:, kt * VG + (2 * pp_in_vg + 1) * 64: kt * VG + (2 * pp_in_vg + 1) * 64 + 64],
                        pt_o[ip][:], start=(kt == 0), stop=(kt == NTT - 1)))
                    guards["pt"][ip] = p.t.bar(mm)

                for qc in range(NCH):
                    OE.acquire(p.t)
                    OO.acquire(p.t)
                    pend = None
                    for kt in range(NTT):
                        i = kt % 2
                        ip = kt % 3
                        SE_ = B[i]
                        SO_ = B[2 + i]
                        p.t.wait(ve, ve_beta)
                        SE_.acquire(p.t)
                        SO_.acquire(p.t)
                        p.t.emit(lambda t, kt=kt, qc=qc, SE_=SE_, pb=pb: t.matmul(
                            SE_.t[:], k2[pb][0:64, kt * 128:(kt + 1) * 128],
                            q2[pb][0:64, qc * CW:(qc + 1) * CW],
                            start=True, stop=False, tile_position=(0, 0)))
                        p.t.emit(lambda t, kt=kt, qc=qc, SO_=SO_, pb=pb: t.matmul(
                            SO_.t[:], k2[pb][64:128, kt * 128:(kt + 1) * 128],
                            q2[pb][64:128, qc * CW:(qc + 1) * CW],
                            start=True, stop=False, tile_position=(64, 0)))
                        p.t.emit(lambda t, kt=kt, qc=qc, SE_=SE_, pb=pb: t.matmul(
                            SE_.t[:], negE[0:NBLK, kt * 128:(kt + 1) * 128],
                            btr2[pb][0:NBLK, qc * CW:(qc + 1) * CW],
                            start=False, stop=True))
                        p.t.emit(lambda t, kt=kt, qc=qc, SO_=SO_, pb=pb: t.matmul(
                            SO_.t[:], negE[64:64 + NBLK, kt * 128:(kt + 1) * 128],
                            btr2[pb][64:64 + NBLK, qc * CW:(qc + 1) * CW],
                            start=False, stop=True))
                        mm_s = p.t.bar(mm)
                        p.s.wait(mm, mm_s)
                        if guards["pt"][ip] is not None:
                            p.s.wait(mm, guards["pt"][ip])
                        p.s.emit(lambda s, ip=ip, SE_=SE_: s.activation(pt_e[ip][:], SE_.t[:], AF.Exp))
                        p.s.emit(lambda s, ip=ip, SO_=SO_: s.activation(pt_o[ip][:], SO_.t[:], AF.Exp))
                        sc_p = p.s.bar(sc)
                        SE_.guard(sc, sc_p)
                        SO_.guard(sc, sc_p)
                        if pend is not None:
                            emit_o(pend)
                        pend = (kt, ip, sc_p)
                        if kt % 2 == 1:
                            yield
                    emit_o(pend)
                    mm_o = mm.n
                    p.s.wait(mm, mm_o)
                    if guards["opair"] is not None:
                        p.s.wait(mm, guards["opair"])
                    p.s.emit(lambda s, qc=qc: s.activation(
                        o_pair[0:64, qc * CW:(qc + 1) * CW], OE.t[0:64, :], AF.Copy))
                    p.s.emit(lambda s, qc=qc: s.activation(
                        o_pair[64:128, qc * CW:(qc + 1) * CW], OO.t[0:64, :], AF.Copy))
                    sc_on = p.s.bar(sc)
                    OE.guard(sc, sc_on)
                    OO.guard(sc, sc_on)
                    yield
                guards["q2"][pb] = mm.n
                guards["btr"][pb] = mm.n
                sc_opair = sc.n
                h_wo = ring.load(p, d_wo[pg * 128:(pg + 1) * 128, :], 128, 1, D)
                for m in range(KT):
                    for c in range(NCH):
                        WB_ = B[(m * NCH + c) % 2]
                        ring.use(p.t, h_wo)
                        p.t.wait(sc, sc_opair)
                        WB_.acquire(p.t)
                        p.t.emit(lambda t, m=m, c=c, h=h_wo, WB_=WB_: t.matmul(
                            WB_.t[:], ring.lhsT(h, 0, m * 128, 128),
                            o_pair[:, c * CW:(c + 1) * CW], start=True, stop=True))
                        mm_wo = p.t.bar(mm)
                        p.v.wait(mm, mm_wo)
                        p.v.emit(lambda v, m=m, c=c, WB_=WB_: v.tensor_add(
                            xv(m, c), xv(m, c), WB_.t[:]))
                        ve_wo = p.v.bar(ve)
                        WB_.guard(ve, ve_wo)
                    if m % 2 == 1:
                        yield
                ring.release(p.t, h_wo)
                guards["opair"] = mm.n
                guards["vt"][vb_] = mm.n

            def run_interleaved(a, b):
                gens = [g for g in (a, b) if g is not None]
                while gens:
                    nxt = []
                    for g in gens:
                        try:
                            next(g)
                            nxt.append(g)
                        except StopIteration:
                            pass
                    gens = nxt

            NPTOT = NH // 2
            for u in range(NPTOT + 1):
                pre = None
                if u < NPTOT:
                    if u % 2 == 0:
                        def pre_gen(u=u):
                            yield from gen_vprod(u // 2)
                            yield from gen_prep(u)
                        pre = pre_gen()
                    else:
                        pre = gen_prep(u)
                stg = gen_st(u - 1) if u > 0 else None
                run_interleaved(pre, stg)
            if "x1a" in debug:
                dump_dbg("x1a", x_sb, [D, T], (ve, ve.n))
            p.flush("attn")

        with ExitStack() as ctx:
            e = ctx.enter_context
            dbg_fence()
            MQ = D // 128
            hh1 = e(nc.sbuf_tensor("hh1", [128, MQ * T], BF16))
            H1 = [Bank(e(nc.psum_tensor(f"f_h1_{i}", [128, CW], F32))) for i in range(2)]
            YB = [Bank(e(nc.psum_tensor(f"f_yb_{i}", [128, CW], F32))) for i in range(2)]
            GB = Bank(e(nc.psum_tensor("f_gb", [128, CW], F32)))
            SLW = min(CW, D)
            NSL = D // SLW
            hh_guard = [None]
            b2_guard = [None, None]
            nb2 = [0]
            for ei in range(NE):
                bi = ei % 2
                if b2_guard[bi] is not None:
                    p.g.wait(s_b2f, b2_guard[bi])
                nb2[0] += 1

                def ldb2(g, ei=ei, bi=bi):
                    g.dma_start(out=b2cur[bi][:],
                                in_=d_b2sel[ei * NE:(ei + 1) * NE, :]).then_inc(s_b2, 16)
                p.g.ops.append(ldb2)
                nb2_here = nb2[0]
                for c in range(NCH):
                    GB.acquire(p.t)
                    p.t.wait(ve, VE_CONST)
                    p.t.emit(lambda t, ei=ei, c=c: t.matmul(
                        GB.t[:], ebc[:, ei * 128:(ei + 1) * 128],
                        gt[0:8, c * CW:(c + 1) * CW], start=True, stop=True))
                    mm_gb = p.t.bar(mm)
                    p.v.wait(mm, mm_gb)
                    p.v.drain()
                    p.v.emit(lambda v, c=c: v.tensor_copy(
                        gb_sb[:, c * CW:(c + 1) * CW], GB.t[:]))
                    ve_gb = p.v.bar(ve)
                    GB.guard(ve, ve_gb)
                for fq in range(4):
                    for sl in range(NSL):
                        h_w1 = ring.load(p, d_w1[ei * D:(ei + 1) * D,
                                                 fq * D + sl * SLW: fq * D + (sl + 1) * SLW],
                                         128, KT, SLW)
                        for ml in range(SLW // 128):
                            m = sl * (SLW // 128) + ml
                            for c in range(NCH):
                                i = (m * NCH + c) % 2
                                ring.use(p.t, h_w1)
                                H1[i].acquire(p.t)
                                for k in range(KT):
                                    p.t.emit(lambda t, k=k, ml=ml, c=c, i=i, h=h_w1: t.matmul(
                                        H1[i].t[:], ring.lhsT(h, k, ml * 128, 128), xnv(k, c),
                                        start=(k == 0), stop=(k == KT - 1)))
                                mm_h = p.t.bar(mm)
                                p.s.wait(mm, mm_h)
                                if hh_guard[0] is not None:
                                    p.s.wait(mm, hh_guard[0])
                                bcol = ei * (F1 // 128) + fq * MQ + m
                                p.s.emit(lambda s, i=i, m=m, c=c, bcol=bcol: s.activation(
                                    hh1[:, m * T + c * CW: m * T + (c + 1) * CW],
                                    H1[i].t[:], AF.Gelu, bias=b1_sb[:, bcol:bcol + 1]))
                                sc_h = p.s.bar(sc)
                                H1[i].guard(sc, sc_h)
                                p.v.wait(sc, sc_h)
                                p.v.emit(lambda v, m=m, c=c: v.tensor_mul(
                                    hh1[:, m * T + c * CW: m * T + (c + 1) * CW],
                                    hh1[:, m * T + c * CW: m * T + (c + 1) * CW],
                                    gb_sb[:, c * CW:(c + 1) * CW]))
                                p.v.drain()
                        ring.release(p.t, h_w1)
                    ve_hh = p.v.bar(ve)
                    for sl in range(NSL):
                        h_w2 = ring.load(p, d_w2[ei * F1 + fq * D: ei * F1 + (fq + 1) * D,
                                                 sl * SLW:(sl + 1) * SLW],
                                         128, KT, SLW)
                        for ml in range(SLW // 128):
                            mg = sl * (SLW // 128) + ml
                            for c in range(NCH):
                                i = (mg * NCH + c) % 2
                                ring.use(p.t, h_w2)
                                p.t.wait(ve, ve_hh)
                                YB[i].acquire(p.t)
                                for k in range(KT):
                                    p.t.emit(lambda t, k=k, ml=ml, c=c, i=i, h=h_w2, fq=fq: t.matmul(
                                        YB[i].t[:], ring.lhsT(h, k, ml * 128, 128),
                                        hh1[:, k * T + c * CW: k * T + (c + 1) * CW],
                                        start=(k == 0), stop=(fq != 0 and k == KT - 1)))
                                if fq == 0:
                                    p.t.wait_dma(s_b2, 16 * nb2_here)
                                    p.t.emit(lambda t, mg=mg, c=c, i=i, bi=bi: t.matmul(
                                        YB[i].t[:], b2cur[bi][0:8, mg * 128:(mg + 1) * 128],
                                        gt[0:8, c * CW:(c + 1) * CW],
                                        start=False, stop=True))
                                mm_y = p.t.bar(mm)
                                p.v.wait(mm, mm_y)
                                p.v.emit(lambda v, mg=mg, c=c, i=i: v.tensor_add(
                                    xv(mg, c), xv(mg, c), YB[i].t[:]))
                                ve_y = p.v.bar(ve)
                                YB[i].guard(ve, ve_y)
                        ring.release(p.t, h_w2)
                    hh_guard[0] = mm.n
                b2_guard[bi] = p.t.bar(s_b2f)
            p.flush("l1ffn")

        def _fin(s):
            s.dma_start(out=d_y.rearrange("(k p) t -> p k t", p=128),
                        in_=x_sb[:].rearrange("p (k t) -> p k t", k=KT)).then_inc(s_out, 16)
            s.wait_ge(s_out, 16)
            if n_out[0] > 1:
                s.wait_ge(s_dbg, 16 * (n_out[0] - 1))
        p.sy.ops.append(_fin)
        p.flush("fin")

    return nc



def _prep_weights(i, cfg):
    D = cfg["D"]
    F1 = 4 * D
    c = np.ascontiguousarray
    PERM = np.concatenate([np.arange(DS),
                           DS + 2 * np.arange(DS), DS + 1 + 2 * np.arange(DS),
                           3 * DS + 2 * np.arange(DS), 3 * DS + 1 + 2 * np.arange(DS)])
    inpw = c(i["L0_attn_in_proj_w"][PERM].T)
    inpb = np.zeros((3, 128), np.float32)
    inpb.reshape(-1)[:5 * DS] = i["L0_attn_in_proj_b"][PERM]
    inpb = c(inpb.T)
    convw = c(i["L0_attn_conv_w"][:, 0, :].reshape(D // 128, 128, DCONV)
              .transpose(1, 0, 2).reshape(128, -1))
    convb = c(i["L0_attn_conv_b"].reshape(D // 128, 128).T)
    alog = c(i["L0_attn_A_log"].reshape(DS, 1))
    outpw = c(i["L0_attn_out_proj_w"].T)
    outpb = c(i["L0_attn_out_proj_b"].reshape(1, D))
    gatew = c(i["L0_ffn_gate_w"].T)
    gbias_tok = c(np.broadcast_to(i["L0_ffn_gate_bias"][None, :], (128, NE)).astype(np.float32))
    routerw = c(i["L1_ffn_router_w"].T)
    w1t = i["L0_ffn_w1"].transpose(0, 2, 1)
    w3t = i["L0_ffn_w3"].transpose(0, 2, 1)
    w2t = i["L0_ffn_w2"].transpose(0, 2, 1)
    sw1t = i["L0_ffn_sw1"].T
    sw3t = i["L0_ffn_sw3"].T
    sw2t = i["L0_ffn_sw2"].T
    moe1 = c(np.concatenate([w1t.reshape(8 * D, D), sw1t[:, :D], sw1t[:, D:]], axis=0))
    moe3 = c(np.concatenate([w3t.reshape(8 * D, D), sw3t[:, :D], sw3t[:, D:]], axis=0))
    moe2 = c(np.concatenate([w2t.reshape(8 * D, D), sw2t[:D, :], sw2t[D:, :]], axis=0))
    l1w1 = c(i["L1_ffn_w1"].transpose(0, 2, 1).reshape(NE * D, F1))
    b1 = c(i["L1_ffn_b1"].reshape(NE, F1 // 128, 128).transpose(2, 0, 1).reshape(128, -1))
    l1w2 = c(i["L1_ffn_w2"].transpose(0, 2, 1).reshape(NE * F1, D))
    b2sel = np.zeros((NE, NE * D), np.float32)
    for e in range(NE):
        b2sel[e, e * D:(e + 1) * D] = i["L1_ffn_b2"][e]
    b2sel = b2sel.reshape(NE * NE, D)
    return dict(convw=convw, convb=convb, inpw=inpw, inpb=inpb, alog=alog,
                outpw=outpw, outpb=outpb, gatew=gatew, gbias_tok=gbias_tok,
                routerw=routerw, moe1=moe1, moe3=moe3, moe2=moe2,
                wq=c(i["L1_attn_Wq"]), wk=c(i["L1_attn_Wk"]),
                wv=c(i["L1_attn_Wv"]), wo=c(i["L1_attn_Wo"]),
                l1w1=l1w1, l1b1=b1, l1w2=l1w2, b2sel=b2sel)


def kernel(**inputs):
    cfg = CFG_FULL
    x = inputs["L0_attn_x"]
    B = x.shape[0]
    wmap = _prep_weights(inputs, cfg)
    in_maps = []
    for b in range(B):
        m = dict(wmap)
        m["xt"] = np.ascontiguousarray(np.asarray(x[b]).T)
        in_maps.append(m)
    nc = build(cfg)
    res = run_bass_kernel_spmd(nc, in_maps, core_ids=list(range(B)))
    out = np.stack([res.results[b]["yt"].T for b in range(B)], axis=0)
    return np.ascontiguousarray(out.astype(np.float32))
